# version 32
# baseline (speedup 1.0000x reference)
import numpy as np
import ml_dtypes
from contextlib import ExitStack

import concourse.bass as bass
import concourse.mybir as mybir
from concourse.bass_utils import run_bass_kernel_spmd
from concourse.alu_op_type import AluOpType as OP

F32 = mybir.dt.float32
BF16 = mybir.dt.bfloat16
U8 = mybir.dt.uint8
AF = mybir.ActivationFunctionType

SEQ = 2
T = 2048
D = 1024
NPOS = SEQ * T
KJ = 256
F0 = 4096
F1 = 4096
LS = 8
CQ = 256
L2S = 8
C2 = 32
NTC = T // 128
NFT = NPOS // 512
EPS_LN = 1e-5
EPS_RMS = 1.1920929e-07
RD = 1.0 / D


class Sched:
    ENGS = ("sync", "scalar", "vector", "tensor", "gpsimd")

    def __init__(self):
        self.q = {e: [] for e in self.ENGS}
        self.cnt = {}
        self.waited = {}

    def bind(self, sems):
        self.sems = sems

    def wait(self, eng, dep):
        if dep is None:
            return
        deps = dep if isinstance(dep, list) else [dep]
        for d in deps:
            if d is None:
                continue
            semname, val = d
            if val <= 0:
                continue
            key = (eng, semname)
            if self.waited.get(key, 0) >= val:
                continue
            self.waited[key] = val
            h = self.sems[semname]
            self.q[eng].append(lambda e, h=h, v=val: e.wait_ge(h, v))

    def op(self, eng, fn, inc=None, n=1):
        if inc is None:
            self.q[eng].append(fn)
            return None
        self.cnt[inc] = self.cnt.get(inc, 0) + n
        newval = self.cnt[inc]

        def g(e, fn=fn, inc=inc, n=n):
            fn(e).then_inc(self.sems[inc], n)
        self.q[eng].append(g)
        return (inc, newval)

    def val(self, sem):
        return (sem, self.cnt.get(sem, 0))


def build(dbg=False, barriers=()):
    nc = bass.Bass("TRN2", target_bir_lowering=False)

    xin = nc.dram_tensor("xin", [NPOS, D], F32, kind="ExternalInput").ap()
    wmt = nc.dram_tensor("wmt", [D, D], F32, kind="ExternalInput").ap()
    wpt = nc.dram_tensor("wpt", [512, D], F32, kind="ExternalInput").ap()
    vv = nc.dram_tensor("vv", [KJ, 2], F32, kind="ExternalInput").ap()
    osr = nc.dram_tensor("osr", [128, D], F32, kind="ExternalInput").ap()
    idin = nc.dram_tensor("idin", [128, 128], F32, kind="ExternalInput").ap()
    k0t = nc.dram_tensor("k0t", [D, F0], U8, kind="ExternalInput").ap()
    k0b = nc.dram_tensor("k0b", [D, F0], U8, kind="ExternalInput").ap()
    k1u = nc.dram_tensor("k1u", [F0, F1], U8, kind="ExternalInput").ap()
    k2u = nc.dram_tensor("k2u", [F1, D], U8, kind="ExternalInput").ap()
    scin = nc.dram_tensor("scin", [128, 3], F32, kind="ExternalInput").ap()
    out = nc.dram_tensor("out", [NPOS, D], F32, kind="ExternalOutput").ap()

    x2s = nc.dram_tensor("x2s", [NPOS, D], F32,
                         kind="ExternalOutput" if dbg else "Internal").ap()
    wkind = "ExternalOutput" if dbg else "Internal"
    w0s = nc.dram_tensor("w0s", [8, 128, F0], BF16, kind=wkind).ap()
    w1s = nc.dram_tensor("w1s", [32, 128, F1], BF16, kind=wkind).ap()
    w2s = nc.dram_tensor("w2s", [32, 128, D], BF16, kind=wkind).ap()
    dM = dH = dXN = dMP = dH0 = dH1 = None
    if dbg:
        dM = nc.dram_tensor("dM", [128, 8 * T], F32, kind="ExternalOutput").ap()
        dH = nc.dram_tensor("dH", [128, 4 * T], F32, kind="ExternalOutput").ap()
        dXN = nc.dram_tensor("dXN", [128, 8 * T], BF16, kind="ExternalOutput").ap()
        dMP = nc.dram_tensor("dMP", [128, 8 * T], F32, kind="ExternalOutput").ap()
        dH0 = nc.dram_tensor("dH0", [128, 32 * 512], BF16, kind="ExternalOutput").ap()
        dH1 = nc.dram_tensor("dH1", [128, 32 * 512], BF16, kind="ExternalOutput").ap()
        dX3h = nc.dram_tensor("dX3h", [128, 8 * 512], BF16, kind="ExternalOutput").ap()
        dX3l = nc.dram_tensor("dX3l", [128, 8 * 512], BF16, kind="ExternalOutput").ap()

    xr = xin.rearrange("(se q s) d -> se s q d", se=SEQ, s=LS)
    outr = out.rearrange("(se q s) d -> se s q d", se=SEQ, s=LS)

    s = Sched()
    _uid = [0]

    def uname(base):
        _uid[0] += 1
        return f"{base}_{_uid[0]}"

    with ExitStack() as ctx:
        def sb(name, shape, dt):
            return ctx.enter_context(nc.sbuf_tensor(name, shape, dt))

        PSF = ctx.enter_context(nc.psum_tensor("psf", [128, 4096], F32))
        PS = [PSF[:, i * 1024:(i + 1) * 1024] for i in range(4)]

        def BANK(k):
            return PSF[:, k * 512:(k + 1) * 512]

        def BANKPE(k):
            return PSF[:, k * 512:k * 512 + 128]

        def PSALL():
            return PSF[:].rearrange("p (b c) -> p b c", b=8)[:, :, 0:128]

        SEMNAMES = ("sLD", "sPL", "sPE", "sACT", "sDVE", "sCST", "sPKa", "sPKb",
                    "sPSa", "sPSb", "sWT", "sXQa", "sXQb", "sX2W", "sW0a", "sW0b",
                    "sW1a", "sW1b", "sW2a", "sW2b", "sX2a", "sX2b", "sOUTa", "sOUTb",
                    "sDBG", "sW0c", "sW0d", "sW1c", "sW1d", "sW2c", "sW2d",
                    "sPKc", "sPKd", "sPSc", "sPSd", "sPKe", "sPKf", "sPSe", "sPSf")
        sems = {n: ctx.enter_context(nc.semaphore(n)) for n in SEMNAMES}
        s.bind(sems)

        bank_prod = [None] * 8
        bank_cons = [[] for _ in range(8)]

        def bank_acquire(k):
            s.wait("tensor", bank_cons[k])
            bank_cons[k] = []

        def bank_read(eng, k, fn, inc):
            s.wait(eng, [bank_prod[k]])
            dep = s.op(eng, fn, inc=inc)
            bank_cons[k].append(dep)
            return dep

        ident = sb("ident", [128, 128], F32)
        vt = sb("vt", [128, 4], F32)
        scs = sb("scs", [128, 8], F32)
        stripA = sb("stripA", [128, 32], F32)
        stripB = sb("stripB", [128, 32], F32)
        stw_g = sb("stw", [128, 32], F32)
        dum = sb("dum", [128, D], F32)

        d_id = s.op("sync", lambda e: e.dma_start(out=ident[:], in_=idin), inc="sCST", n=16)
        d_sc = s.op("sync", lambda e: e.dma_start(out=scs[:, 0:3], in_=scin), inc="sCST", n=16)
        d_vt = []
        for a in range(2):
            for jg in range(2):
                d_vt.append(s.op("sync", lambda e, a=a, jg=jg: e.dma_start(
                    out=vt[:, a * 2 + jg:a * 2 + jg + 1],
                    in_=vv[jg * 128:(jg + 1) * 128, a:a + 1],
                    allow_slow_non_contiguous=True), inc="sCST", n=16))
        cst_all = ("sCST", 6 * 16)
        s.wait("vector", [cst_all])
        s.op("vector", lambda e: e.tensor_scalar(scs[:, 3:4], scs[:, 0:1], 4.0, None, OP.mult))
        s.op("vector", lambda e: e.tensor_scalar(scs[:, 4:5], scs[:, 1:2], 2.0, None, OP.mult))
        d_scs = s.op("vector", lambda e: e.tensor_tensor(scs[:, 5:6], scs[:, 2:3], scs[:, 2:3], OP.mult), inc="sDVE")

        def vdrain():
            s.op("vector", lambda e: e.drain())

        def BAR(tag):
            if tag not in barriers:
                return
            deps = [s.val(x) for x in SEMNAMES]
            for eng in ("scalar", "vector", "tensor", "gpsimd", "sync"):
                s.wait(eng, deps)

        def BARX(tag, eng, semnames):
            if tag not in barriers:
                return
            s.wait(eng, [s.val(x) for x in semnames])

        def FBAR():
            deps = [s.val(x) for x in SEMNAMES]
            for eng in ("scalar", "vector", "tensor", "gpsimd", "sync"):
                s.wait(eng, deps)

        def ln_strips(src_ap, W, idx, mode, SB=None, mid=None):
            stw = SB if SB is not None else stw_g
            a1_ = s.op("scalar", lambda e, stw=stw: e.activation(
                dum[:], src_ap, AF.Copy, accum_out=stw[:, W + 0:W + 1]), inc="sACT")
            a2_ = s.op("scalar", lambda e, stw=stw: e.activation(
                dum[:], src_ap, AF.Square, accum_out=stw[:, W + 1:W + 2]), inc="sACT")
            if mid is not None:
                mid()
            s.wait("vector", [a1_, a2_])
            s.op("vector", lambda e: e.tensor_scalar(stw[:, W + 2:W + 3], stw[:, W + 0:W + 1], RD, None, OP.mult))
            s.op("vector", lambda e: e.tensor_scalar(stw[:, W + 3:W + 4], stw[:, W + 1:W + 2], RD, None, OP.mult))
            s.op("vector", lambda e: e.tensor_tensor(stw[:, W + 4:W + 5], stw[:, W + 2:W + 3], stw[:, W + 2:W + 3], OP.mult))
            vdrain()
            s.op("vector", lambda e: e.tensor_tensor(stw[:, W + 5:W + 6], stw[:, W + 3:W + 4], stw[:, W + 4:W + 5], OP.subtract))
            vdrain()
            if mode == "s2":
                s.wait("vector", [d_scs])
                dv1 = s.op("vector", lambda e: e.tensor_scalar(
                    stw[:, W + 6:W + 7], stw[:, W + 5:W + 6], scs[:, 5:6], EPS_LN, OP.mult, OP.add), inc="sDVE")
                s.wait("scalar", [dv1])
                da1 = s.op("scalar", lambda e: e.activation(stw[:, W + 7:W + 8], stw[:, W + 6:W + 7], AF.Sqrt), inc="sACT")
                s.wait("vector", [da1])
                s.op("vector", lambda e: e.reciprocal(stw[:, W + 8:W + 9], stw[:, W + 7:W + 8]))
                vdrain()
                s.op("vector", lambda e: e.tensor_tensor(stw[:, W + 8:W + 9], stw[:, W + 8:W + 9], scs[:, 2:3], OP.mult))
                vdrain()
                dv3 = s.op("vector", lambda e: e.scalar_tensor_tensor(
                    stw[:, W + 9:W + 10], stw[:, W + 2:W + 3], -1.0, stw[:, W + 8:W + 9], OP.mult, OP.mult), inc="sDVE")
                return dv3, stw[:, W + 8:W + 9], stw[:, W + 9:W + 10]
            if mode == "plain":
                dv1 = s.op("vector", lambda e: e.tensor_scalar(
                    stw[:, W + 6:W + 7], stw[:, W + 5:W + 6], EPS_LN, None, OP.add), inc="sDVE")
                s.wait("scalar", [dv1])
                da1 = s.op("scalar", lambda e: e.activation(stw[:, W + 7:W + 8], stw[:, W + 6:W + 7], AF.Sqrt), inc="sACT")
                s.wait("vector", [da1])
                s.op("vector", lambda e: e.reciprocal(stw[:, W + 8:W + 9], stw[:, W + 7:W + 8]))
                vdrain()
                dv3 = s.op("vector", lambda e: e.scalar_tensor_tensor(
                    stw[:, W + 9:W + 10], stw[:, W + 2:W + 3], -1.0, stw[:, W + 8:W + 9], OP.mult, OP.mult), inc="sDVE")
                return dv3, stw[:, W + 8:W + 9], stw[:, W + 9:W + 10]
            s.op("vector", lambda e: e.tensor_scalar(stw[:, W + 6:W + 7], stw[:, W + 5:W + 6], EPS_LN, None, OP.add))
            vdrain()
            s.op("vector", lambda e: e.reciprocal(stw[:, W + 8:W + 9], stw[:, W + 6:W + 7]))
            vdrain()
            s.op("vector", lambda e: e.tensor_tensor(stw[:, W + 9:W + 10], stw[:, W + 5:W + 6], stw[:, W + 8:W + 9], OP.mult))
            vdrain()
            dv1 = s.op("vector", lambda e: e.tensor_scalar(stw[:, W + 10:W + 11], stw[:, W + 9:W + 10], EPS_RMS, None, OP.add), inc="sDVE")
            s.wait("scalar", [dv1])
            s.op("scalar", lambda e: e.activation(stw[:, W + 7:W + 8], stw[:, W + 6:W + 7], AF.Sqrt))
            da1 = s.op("scalar", lambda e: e.activation(stw[:, W + 11:W + 12], stw[:, W + 10:W + 11], AF.Sqrt), inc="sACT")
            s.wait("vector", [da1])
            s.op("vector", lambda e, idx=idx: e.reciprocal(stripA[:, idx:idx + 1], stw[:, W + 7:W + 8]))
            s.op("vector", lambda e: e.reciprocal(stw[:, W + 12:W + 13], stw[:, W + 11:W + 12]))
            vdrain()
            s.op("vector", lambda e, idx=idx: e.tensor_tensor(stw[:, W + 13:W + 14], stripA[:, idx:idx + 1], stw[:, W + 12:W + 13], OP.mult))
            s.op("vector", lambda e, idx=idx: e.scalar_tensor_tensor(
                stripB[:, idx:idx + 1], stw[:, W + 2:W + 3], -1.0, stripA[:, idx:idx + 1], OP.mult, OP.mult))
            vdrain()
            dv3 = s.op("vector", lambda e: e.scalar_tensor_tensor(
                stw[:, W + 14:W + 15], stw[:, W + 2:W + 3], -1.0, stw[:, W + 13:W + 14], OP.mult, OP.mult), inc="sDVE")
            return dv3, stw[:, W + 13:W + 14], stw[:, W + 14:W + 15]

        with ExitStack() as prep:
            u8a = prep.enter_context(nc.sbuf_tensor("u8a", [128, 6 * F0], U8))
            u8b = prep.enter_context(nc.sbuf_tensor("u8b", [128, 2 * F0], U8))
            wcv = prep.enter_context(nc.sbuf_tensor("wcv", [128, 6 * F0], BF16))
            pl_by_par = [None] * 6
            PKS_ = ("sPKa", "sPKb", "sPKc", "sPKd", "sPKe", "sPKf")
            PSS_ = ("sPSa", "sPSb", "sPSc", "sPSd", "sPSe", "sPSf")
            for dc in range(8):
                p_ = dc % 2
                s.wait("sync", [pl_by_par[p_]])
                da = s.op("sync", lambda e, dc=dc, p_=p_: e.dma_start(
                    out=u8a[:, p_ * F0:(p_ + 1) * F0], in_=k0t[dc * 128:(dc + 1) * 128, :]),
                    inc=PKS_[p_], n=16)
                db = s.op("sync", lambda e, dc=dc, p_=p_: e.dma_start(
                    out=u8b[:, p_ * F0:(p_ + 1) * F0], in_=k0b[dc * 128:(dc + 1) * 128, :]),
                    inc=PKS_[p_], n=16)
                s.wait("vector", [da, db, pl_by_par[p_]])
                dv = s.op("vector", lambda e, p_=p_: e.tensor_tensor(
                    wcv[:, p_ * F0:(p_ + 1) * F0], u8a[:, p_ * F0:(p_ + 1) * F0],
                    u8b[:, p_ * F0:(p_ + 1) * F0], OP.subtract), inc="sDVE")
                q_ = "gpsimd" if p_ < 1 else "scalar"
                s.wait(q_, [dv])
                pl_by_par[p_] = s.op(q_, lambda e, dc=dc, p_=p_: e.dma_start(
                    out=w0s[dc], in_=wcv[:, p_ * F0:(p_ + 1) * F0]),
                    inc=PSS_[p_], n=16)
            for k in range(32):
                p_ = k % 6
                s.wait("sync", [pl_by_par[p_]])
                dl = s.op("sync", lambda e, k=k, p_=p_: e.dma_start(
                    out=u8a[:, p_ * F0:(p_ + 1) * F0], in_=k1u[k * 128:(k + 1) * 128, :]),
                    inc=PKS_[p_], n=16)
                s.wait("scalar", [dl, pl_by_par[p_]])
                dc_ = s.op("scalar", lambda e, p_=p_: e.activation(
                    wcv[:, p_ * F0:(p_ + 1) * F0], u8a[:, p_ * F0:(p_ + 1) * F0],
                    AF.Copy, bias=-1.0, scale=2.0), inc="sACT")
                q_ = "gpsimd" if p_ < 3 else "scalar"
                s.wait(q_, [dc_])
                pl_by_par[p_] = s.op(q_, lambda e, k=k, p_=p_: e.dma_start(
                    out=w1s[k], in_=wcv[:, p_ * F0:(p_ + 1) * F0]),
                    inc=PSS_[p_], n=16)
            for k in range(32):
                p_ = k % 6
                s.wait("sync", [pl_by_par[p_]])
                dl = s.op("sync", lambda e, k=k, p_=p_: e.dma_start(
                    out=u8a[:, p_ * F0:p_ * F0 + D], in_=k2u[k * 128:(k + 1) * 128, :]),
                    inc=PKS_[p_], n=16)
                s.wait("scalar", [dl, pl_by_par[p_]])
                dc_ = s.op("scalar", lambda e, p_=p_: e.activation(
                    wcv[:, p_ * F0:p_ * F0 + D], u8a[:, p_ * F0:p_ * F0 + D],
                    AF.Copy, bias=-1.0, scale=2.0), inc="sACT")
                q_ = "gpsimd" if p_ < 3 else "scalar"
                s.wait(q_, [dc_])
                pl_by_par[p_] = s.op(q_, lambda e, k=k, p_=p_: e.dma_start(
                    out=w2s[k], in_=wcv[:, p_ * F0:p_ * F0 + D]),
                    inc=PSS_[p_], n=16)
        conv_done = [s.val(x) for x in PSS_]
        FBAR()

        x2w = [None] * 32
        xq_cons = [None, None]

        with ExitStack() as attn:
            def asb(name, shape, dt):
                return attn.enter_context(nc.sbuf_tensor(name, shape, dt))
            wmh = asb("wmh", [128, 8 * D], BF16)
            wml = asb("wml", [128, 8 * D], BF16)
            wph = asb("wph", [128, 4 * D], BF16)
            wpl = asb("wpl", [128, 4 * D], BF16)
            xq = asb("xq", [128, 2 * D], F32)
            xnt = asb("xnt", [128, 2 * D], F32)
            Mt = asb("Mt", [128, 8 * T], F32)

            def Mtile(comp, jg):
                return Mt[:, (comp * 2 + jg) * T:(comp * 2 + jg + 1) * T]

            with ExitStack() as wprep:
                wtmp = wprep.enter_context(nc.sbuf_tensor("wtmp", [128, D], F32))
                ostmp = wprep.enter_context(nc.sbuf_tensor("ostmp", [128, D], F32))
                d_os = s.op("sync", lambda e: e.dma_start(out=ostmp[:], in_=osr), inc="sWT", n=16)
                last = None
                for dc in range(8):
                    s.wait("sync", [last])
                    dl = s.op("sync", lambda e, dc=dc: e.dma_start(
                        out=wtmp[:], in_=wmt[dc * 128:(dc + 1) * 128, :]), inc="sWT", n=16)
                    s.wait("vector", [dl])
                    s.op("vector", lambda e, dc=dc: e.tensor_copy(wmh[:, dc * D:(dc + 1) * D], wtmp[:]))
                    s.op("vector", lambda e: e.drain())
                    last = s.op("vector", lambda e, dc=dc: e.tensor_tensor(
                        wml[:, dc * D:(dc + 1) * D], wtmp[:], wmh[:, dc * D:(dc + 1) * D], OP.subtract), inc="sDVE")
                for cc in range(4):
                    s.wait("sync", [last])
                    dl = s.op("sync", lambda e, cc=cc: e.dma_start(
                        out=wtmp[:], in_=wpt[cc * 128:(cc + 1) * 128, :]), inc="sWT", n=16)
                    s.wait("vector", [dl, d_os])
                    s.op("vector", lambda e: e.tensor_tensor(wtmp[:], wtmp[:], ostmp[:], OP.mult))
                    s.op("vector", lambda e: e.drain())
                    s.op("vector", lambda e, cc=cc: e.tensor_copy(wph[:, cc * D:(cc + 1) * D], wtmp[:]))
                    s.op("vector", lambda e: e.drain())
                    last = s.op("vector", lambda e, cc=cc: e.tensor_tensor(
                        wpl[:, cc * D:(cc + 1) * D], wtmp[:], wph[:, cc * D:(cc + 1) * D], OP.subtract), inc="sDVE")
            w_ready = last

            xnt_cons = [None, None]
            for se in range(SEQ):
                with ExitStack() as sub1:
                    xnh = sub1.enter_context(nc.sbuf_tensor(uname("xnh"), [128, 8 * T], BF16))
                    xnl = sub1.enter_context(nc.sbuf_tensor(uname("xnl"), [128, 8 * T], BF16))
                    FBAR()
                    guard = [s.val("sPE"), s.val("sACT"), s.val("sDVE")]

                    def backA_evac(tc):
                        s.wait("scalar", [bank_prod[7]])
                        dhi = s.op("scalar", lambda e, tc=tc: e.copy(
                            xnh[:].rearrange("p (dc t) -> p dc t", dc=8)[:, :, tc * 128:tc * 128 + 128],
                            PSALL()), inc="sACT")
                        s.wait("vector", [dhi])
                        dlo = s.op("vector", lambda e, tc=tc: e.tensor_tensor(
                            xnl[:].rearrange("p (dc t) -> p dc t", dc=8)[:, :, tc * 128:tc * 128 + 128],
                            PSALL(),
                            xnh[:].rearrange("p (dc t) -> p dc t", dc=8)[:, :, tc * 128:tc * 128 + 128],
                            OP.subtract), inc="sDVE")
                        for dc in range(8):
                            bank_cons[dc] = [dhi, dlo]

                    for tc in range(NTC):
                        par = tc % 2
                        idx = se * 16 + tc
                        W = par * 16
                        s.wait("sync", [xq_cons[par]])
                        dep_x = s.op("sync", lambda e, se=se, tc=tc, par=par: e.dma_start(
                            out=xq[:, par * D:(par + 1) * D],
                            in_=xr[se, tc // 2, (tc % 2) * 128:(tc % 2) * 128 + 128, :]),
                            inc="sXQa" if par == 0 else "sXQb", n=16)
                        s.wait("scalar", [dep_x])
                        if tc == 0:
                            s.wait("scalar", guard)
                            s.wait("vector", guard)
                        mid = (lambda tc=tc: backA_evac(tc - 1)) if tc > 0 else None
                        dv3, sc_ap, bi_ap = ln_strips(xq[:, par * D:(par + 1) * D], W, idx, "rms", mid=mid)
                        s.wait("scalar", [dv3, xnt_cons[par]])
                        dxn = s.op("scalar", lambda e, par=par, sc_ap=sc_ap, bi_ap=bi_ap: e.activation(
                            xnt[:, par * D:(par + 1) * D], xq[:, par * D:(par + 1) * D],
                            AF.Identity, scale=sc_ap, bias=bi_ap), inc="sACT")
                        xq_cons[par] = dxn
                        s.wait("tensor", [dxn, d_id])
                        for dc in range(8):
                            bank_acquire(dc)
                            dtr = s.op("tensor", lambda e, dc=dc, par=par: e.transpose(
                                BANKPE(dc), xnt[:, par * D + dc * 128:par * D + (dc + 1) * 128], ident[:]), inc="sPE")
                            bank_prod[dc] = dtr
                            if dc == 7:
                                xnt_cons[par] = dtr
                    backA_evac(NTC - 1)

                    def head_norm_l1():
                        ms = [Mtile(c, 0) for c in range(4)]
                        mv = [m.rearrange("p (s q) -> p s q", s=LS) for m in ms]
                        HD = T // 2
                        nA = xnt[:, 0:HD]
                        nB = xnt[:, HD:2 * HD]
                        dmult = None
                        for hf in range(2):
                            hs_ = hf * HD
                            s.wait("scalar", [dmult])
                            dn = s.op("scalar", lambda e, ms=ms, hs_=hs_: e.activation(
                                nA, ms[0][:, hs_:hs_ + HD], AF.Square), inc="sACT")
                            dadd = None
                            for c in (1, 2, 3):
                                s.wait("scalar", [dadd])
                                dq = s.op("scalar", lambda e, ms=ms, c=c, hs_=hs_: e.activation(
                                    nB, ms[c][:, hs_:hs_ + HD], AF.Square), inc="sACT")
                                s.wait("vector", [dn, dq])
                                vdrain()
                                dadd = s.op("vector", lambda e: e.tensor_tensor(nA, nA, nB, OP.add), inc="sDVE")
                            vdrain()
                            dnv = s.op("vector", lambda e: e.tensor_scalar(nA, nA, 1e-12, None, OP.max), inc="sDVE")
                            s.wait("scalar", [dnv])
                            s.op("scalar", lambda e: e.activation(nA, nA, AF.Ln))
                            dr = s.op("scalar", lambda e: e.activation(nA, nA, AF.Exp, scale=-0.5), inc="sACT")
                            s.wait("vector", [dr])
                            for c in range(4):
                                dmult = s.op("vector", lambda e, ms=ms, c=c, hs_=hs_: e.tensor_tensor(
                                    ms[c][:, hs_:hs_ + HD], ms[c][:, hs_:hs_ + HD], nA, OP.mult),
                                    inc="sDVE" if c == 3 else None)
                        vdrain()
                        tq_ = [xnt[:, i * CQ:(i + 1) * CQ] for i in range(4)]
                        for st_ in range(1, LS):
                            cur = [v[:, st_, :] for v in mv]
                            prv = [v[:, st_ - 1, :] for v in mv]
                            s.op("vector", lambda e, cur=cur, prv=prv: e.tensor_tensor(tq_[0], cur[0], prv[0], OP.mult))
                            s.op("vector", lambda e, cur=cur, prv=prv: e.tensor_tensor(tq_[1], cur[1], prv[2], OP.mult))
                            s.op("vector", lambda e, cur=cur, prv=prv: e.tensor_tensor(tq_[2], cur[0], prv[1], OP.mult))
                            s.op("vector", lambda e, cur=cur, prv=prv: e.tensor_tensor(tq_[3], cur[1], prv[3], OP.mult))
                            s.op("vector", lambda e, cur=cur: e.tensor_tensor(cur[0], tq_[0], tq_[1], OP.add))
                            s.op("vector", lambda e, cur=cur: e.tensor_tensor(cur[1], tq_[2], tq_[3], OP.add))
                            s.op("vector", lambda e, cur=cur, prv=prv: e.tensor_tensor(tq_[0], cur[2], prv[0], OP.mult))
                            s.op("vector", lambda e, cur=cur, prv=prv: e.tensor_tensor(tq_[1], cur[3], prv[2], OP.mult))
                            s.op("vector", lambda e, cur=cur, prv=prv: e.tensor_tensor(tq_[2], cur[2], prv[1], OP.mult))
                            s.op("vector", lambda e, cur=cur, prv=prv: e.tensor_tensor(tq_[3], cur[3], prv[3], OP.mult))
                            s.op("vector", lambda e, cur=cur: e.tensor_tensor(cur[2], tq_[0], tq_[1], OP.add))
                            s.op("vector", lambda e, cur=cur: e.tensor_tensor(cur[3], tq_[2], tq_[3], OP.add))
                        return s.op("vector", lambda e: e.drain(), inc="sDVE")

                    s.wait("tensor", [w_ready, s.val("sACT"), s.val("sDVE")])
                    for fc in (0, 2, 4, 6, 1, 3, 5, 7):
                        grp = (fc % 2) * 4
                        for tt in range(4):
                            bank_acquire(grp + tt)
                        for dc in range(8):
                            for (wsp, xsp) in ((wmh, xnh), (wmh, xnl), (wml, xnh)):
                                firstpass = (dc == 0) and (wsp is wmh) and (xsp is xnh)
                                lastpass = (dc == 7) and (wsp is wml)
                                for tt in range(4):
                                    dmm = s.op("tensor", lambda e, fc=fc, dc=dc, wsp=wsp, xsp=xsp, tt=tt, st=firstpass, sp=lastpass, grp=grp: e.matmul(
                                        BANK(grp + tt),
                                        wsp[:, dc * D + fc * 128:dc * D + fc * 128 + 128],
                                        xsp[:, dc * T + tt * 512:dc * T + (tt + 1) * 512],
                                        start=st, stop=sp), inc="sPE" if lastpass else None)
                                    if lastpass:
                                        bank_prod[grp + tt] = dmm
                        comp, jg = fc // 2, fc % 2
                        for tt in range(4):
                            bank_read("scalar", grp + tt, lambda e, comp=comp, jg=jg, tt=tt, grp=grp: e.copy(
                                Mtile(comp, jg)[:, tt * 512:(tt + 1) * 512], BANK(grp + tt)), inc="sACT")
                        if fc == 6:
                            jg0_ready = [s.val("sPE"), s.val("sACT")]
                            s.wait("scalar", jg0_ready)
                            s.wait("vector", jg0_ready)
                            d_scan0 = head_norm_l1()
                            xnt_cons[0] = d_scan0
                            xnt_cons[1] = d_scan0
                    if dbg and se == 0:
                        s.wait("gpsimd", [s.val("sACT"), s.val("sDVE")])
                        s.op("gpsimd", lambda e: e.dma_start(out=dXN, in_=xnh[:]), inc="sDBG", n=16)
                        s.op("gpsimd", lambda e: e.dma_start(out=dMP, in_=Mt[:]), inc="sDBG", n=16)
                b_done = [s.val("sPE"), s.val("sACT")]

                with ExitStack() as sub2:
                    def s2b(name, shape, dt):
                        return sub2.enter_context(nc.sbuf_tensor(uname(name), shape, dt))
                    t1 = s2b("t1", [128, CQ], F32)
                    t2 = s2b("t2", [128, CQ], F32)
                    t3 = s2b("t3", [128, CQ], F32)
                    t4 = s2b("t4", [128, CQ], F32)
                    P2 = s2b("P2", [128, 4 * CQ], F32)
                    R3a = s2b("R3a", [128, 4 * C2], F32)
                    R3b = s2b("R3b", [128, 4 * C2], F32)
                    w3 = s2b("w3", [128, 2 * (C2 + 1)], F32)
                    winc = s2b("winc", [128, 2 * CQ], F32)
                    wq = s2b("wq", [128, 2 * CQ], F32)
                    hh = s2b("hh", [128, 4 * T], BF16)
                    hl = s2b("hl", [128, 4 * T], BF16)
                    n2 = s2b("n2", [128, T], F32)
                    sqt = s2b("sqt", [128, T], F32)

                    def htile(tb, a, jg):
                        return tb[:, (a * 2 + jg) * T:(a * 2 + jg + 1) * T]

                    FBAR()
                    s.wait("scalar", b_done)
                    s.wait("vector", b_done)
                    s.wait("vector", d_vt)
                    prev_combine = None
                    for jg in range(2):
                        ms = [Mtile(c, jg) for c in range(4)]
                        mv = [m.rearrange("p (s q) -> p s q", s=LS) for m in ms]
                        if jg == 1:
                            s.wait("scalar", [prev_combine])
                            dn = s.op("scalar", lambda e, ms=ms: e.activation(n2[:], ms[0], AF.Square), inc="sACT")
                            dadd = None
                            for c in (1, 2, 3):
                                s.wait("scalar", [dadd])
                                dq = s.op("scalar", lambda e, ms=ms, c=c: e.activation(sqt[:], ms[c], AF.Square), inc="sACT")
                                s.wait("vector", [dn, dq])
                                vdrain()
                                dadd = s.op("vector", lambda e: e.tensor_tensor(n2[:], n2[:], sqt[:], OP.add), inc="sDVE")
                            vdrain()
                            dnv = s.op("vector", lambda e: e.tensor_scalar(n2[:], n2[:], 1e-12, None, OP.max), inc="sDVE")
                            s.wait("scalar", [dnv])
                            s.op("scalar", lambda e: e.activation(n2[:], n2[:], AF.Ln))
                            dr = s.op("scalar", lambda e: e.activation(n2[:], n2[:], AF.Exp, scale=-0.5), inc="sACT")
                            s.wait("vector", [dr])
                            for c in range(4):
                                s.op("vector", lambda e, ms=ms, c=c: e.tensor_tensor(ms[c], ms[c], n2[:], OP.mult))

                            for st_ in range(1, LS):
                                cur = [v[:, st_, :] for v in mv]
                                prv = [v[:, st_ - 1, :] for v in mv]
                                s.op("vector", lambda e, cur=cur, prv=prv: e.tensor_tensor(t1[:], cur[0], prv[0], OP.mult))
                                s.op("vector", lambda e, cur=cur, prv=prv: e.tensor_tensor(t2[:], cur[1], prv[2], OP.mult))
                                s.op("vector", lambda e, cur=cur, prv=prv: e.tensor_tensor(t3[:], cur[0], prv[1], OP.mult))
                                s.op("vector", lambda e, cur=cur, prv=prv: e.tensor_tensor(t4[:], cur[1], prv[3], OP.mult))
                                s.op("vector", lambda e, cur=cur: e.tensor_tensor(cur[0], t1[:], t2[:], OP.add))
                                s.op("vector", lambda e, cur=cur: e.tensor_tensor(cur[1], t3[:], t4[:], OP.add))
                                s.op("vector", lambda e, cur=cur, prv=prv: e.tensor_tensor(t1[:], cur[2], prv[0], OP.mult))
                                s.op("vector", lambda e, cur=cur, prv=prv: e.tensor_tensor(t2[:], cur[3], prv[2], OP.mult))
                                s.op("vector", lambda e, cur=cur, prv=prv: e.tensor_tensor(t3[:], cur[2], prv[1], OP.mult))
                                s.op("vector", lambda e, cur=cur, prv=prv: e.tensor_tensor(t4[:], cur[3], prv[3], OP.mult))
                                s.op("vector", lambda e, cur=cur: e.tensor_tensor(cur[2], t1[:], t2[:], OP.add))
                                s.op("vector", lambda e, cur=cur: e.tensor_tensor(cur[3], t3[:], t4[:], OP.add))

                        for c in range(4):
                            s.op("vector", lambda e, mv=mv, c=c: e.tensor_copy(
                                P2[:, c * CQ:(c + 1) * CQ], mv[c][:, LS - 1, :]))
                        p2v = [P2[:, c * CQ:(c + 1) * CQ].rearrange("p (q2 s2) -> p q2 s2", s2=L2S)
                               for c in range(4)]
                        for st_ in range(1, L2S):
                            cur = [v[:, :, st_] for v in p2v]
                            prv = [v[:, :, st_ - 1] for v in p2v]
                            s.op("vector", lambda e, cur=cur, prv=prv: e.tensor_tensor(t1[:, 0:C2], cur[0], prv[0], OP.mult))
                            s.op("vector", lambda e, cur=cur, prv=prv: e.tensor_tensor(t2[:, 0:C2], cur[1], prv[2], OP.mult))
                            s.op("vector", lambda e, cur=cur, prv=prv: e.tensor_tensor(t3[:, 0:C2], cur[0], prv[1], OP.mult))
                            s.op("vector", lambda e, cur=cur, prv=prv: e.tensor_tensor(t4[:, 0:C2], cur[1], prv[3], OP.mult))
                            s.op("vector", lambda e, cur=cur: e.tensor_tensor(cur[0], t1[:, 0:C2], t2[:, 0:C2], OP.add))
                            s.op("vector", lambda e, cur=cur: e.tensor_tensor(cur[1], t3[:, 0:C2], t4[:, 0:C2], OP.add))
                            s.op("vector", lambda e, cur=cur, prv=prv: e.tensor_tensor(t1[:, 0:C2], cur[2], prv[0], OP.mult))
                            s.op("vector", lambda e, cur=cur, prv=prv: e.tensor_tensor(t2[:, 0:C2], cur[3], prv[2], OP.mult))
                            s.op("vector", lambda e, cur=cur, prv=prv: e.tensor_tensor(t3[:, 0:C2], cur[2], prv[1], OP.mult))
                            s.op("vector", lambda e, cur=cur, prv=prv: e.tensor_tensor(t4[:, 0:C2], cur[3], prv[3], OP.mult))
                            s.op("vector", lambda e, cur=cur: e.tensor_tensor(cur[2], t1[:, 0:C2], t2[:, 0:C2], OP.add))
                            s.op("vector", lambda e, cur=cur: e.tensor_tensor(cur[3], t3[:, 0:C2], t4[:, 0:C2], OP.add))

                        for c in range(4):
                            s.op("vector", lambda e, p2v=p2v, c=c: e.tensor_copy(
                                R3a[:, c * C2:(c + 1) * C2], p2v[c][:, :, L2S - 1]))
                        src, dst = R3a, R3b
                        st_ = 1
                        while st_ < C2:
                            n = C2 - st_
                            for c in range(4):
                                s.op("vector", lambda e, src=src, dst=dst, c=c, st_=st_: e.tensor_copy(
                                    dst[:, c * C2:c * C2 + st_], src[:, c * C2:c * C2 + st_]))

                            def A(c, lo, ln, src=src):
                                return src[:, c * C2 + lo:c * C2 + lo + ln]

                            def Bv(c, lo, ln, dst=dst):
                                return dst[:, c * C2 + lo:c * C2 + lo + ln]
                            s.op("vector", lambda e, A=A, st_=st_, n=n: e.tensor_tensor(t1[:, 0:n], A(0, st_, n), A(0, 0, n), OP.mult))
                            s.op("vector", lambda e, A=A, st_=st_, n=n: e.tensor_tensor(t2[:, 0:n], A(1, st_, n), A(2, 0, n), OP.mult))
                            s.op("vector", lambda e, A=A, st_=st_, n=n: e.tensor_tensor(t3[:, 0:n], A(0, st_, n), A(1, 0, n), OP.mult))
                            s.op("vector", lambda e, A=A, st_=st_, n=n: e.tensor_tensor(t4[:, 0:n], A(1, st_, n), A(3, 0, n), OP.mult))
                            s.op("vector", lambda e, Bv=Bv, st_=st_, n=n: e.tensor_tensor(Bv(0, st_, n), t1[:, 0:n], t2[:, 0:n], OP.add))
                            s.op("vector", lambda e, Bv=Bv, st_=st_, n=n: e.tensor_tensor(Bv(1, st_, n), t3[:, 0:n], t4[:, 0:n], OP.add))
                            s.op("vector", lambda e, A=A, st_=st_, n=n: e.tensor_tensor(t1[:, 0:n], A(2, st_, n), A(0, 0, n), OP.mult))
                            s.op("vector", lambda e, A=A, st_=st_, n=n: e.tensor_tensor(t2[:, 0:n], A(3, st_, n), A(2, 0, n), OP.mult))
                            s.op("vector", lambda e, A=A, st_=st_, n=n: e.tensor_tensor(t3[:, 0:n], A(2, st_, n), A(1, 0, n), OP.mult))
                            s.op("vector", lambda e, A=A, st_=st_, n=n: e.tensor_tensor(t4[:, 0:n], A(3, st_, n), A(3, 0, n), OP.mult))
                            s.op("vector", lambda e, Bv=Bv, st_=st_, n=n: e.tensor_tensor(Bv(2, st_, n), t1[:, 0:n], t2[:, 0:n], OP.add))
                            s.op("vector", lambda e, Bv=Bv, st_=st_, n=n: e.tensor_tensor(Bv(3, st_, n), t3[:, 0:n], t4[:, 0:n], OP.add))
                            src, dst = dst, src
                            st_ *= 2
                        R3f = src

                        v0ap = vt[:, jg:jg + 1]
                        v1ap = vt[:, 2 + jg:2 + jg + 1]
                        for a in range(2):
                            base = a * (C2 + 1)
                            vap = v0ap if a == 0 else v1ap
                            s.op("vector", lambda e, base=base, vap=vap: e.tensor_copy(
                                w3[:, base:base + 1], vap))
                            s.op("vector", lambda e, a=a, base=base, R3f=R3f, v0ap=v0ap: e.tensor_scalar(
                                w3[:, base + 1:base + 1 + C2], R3f[:, (a * 2) * C2:(a * 2 + 1) * C2],
                                v0ap, None, OP.mult))
                            vdrain()
                            s.op("vector", lambda e, a=a, base=base, R3f=R3f, v1ap=v1ap: e.scalar_tensor_tensor(
                                w3[:, base + 1:base + 1 + C2], R3f[:, (a * 2 + 1) * C2:(a * 2 + 2) * C2], v1ap,
                                w3[:, base + 1:base + 1 + C2], OP.mult, OP.add))

                        w0b = w3[:, 0:C2].unsqueeze(2).broadcast_to((128, C2, L2S))
                        w1b = w3[:, (C2 + 1):(C2 + 1) + C2].unsqueeze(2).broadcast_to((128, C2, L2S))
                        for a in range(2):
                            ov = winc[:, a * CQ:(a + 1) * CQ]
                            s.op("vector", lambda e, a=a, ov=ov, p2v=p2v, w0b=w0b: e.tensor_tensor(
                                ov.rearrange("p (q2 s2) -> p q2 s2", s2=L2S), p2v[a * 2], w0b, OP.mult))
                            s.op("vector", lambda e, a=a, p2v=p2v, w1b=w1b: e.tensor_tensor(
                                t1[:, 0:CQ].rearrange("p (q2 s2) -> p q2 s2", s2=L2S), p2v[a * 2 + 1], w1b, OP.mult))
                            vdrain()
                            s.op("vector", lambda e, ov=ov: e.tensor_tensor(ov, ov, t1[:, 0:CQ], OP.add))

                        for a in range(2):
                            base = a * CQ
                            s.op("vector", lambda e, a=a, base=base: e.tensor_copy(
                                wq[:, base + 1:base + CQ], winc[:, a * CQ:a * CQ + CQ - 1]))
                            s.op("vector", lambda e, a=a, base=base: e.tensor_copy(
                                wq[:, base:base + 1], w3[:, a * (C2 + 1):a * (C2 + 1) + 1]))
                            s.op("vector", lambda e, a=a, base=base: e.tensor_copy(
                                wq[:, base:base + CQ].rearrange("p (q2 s2) -> p q2 s2", s2=L2S)[:, 1:C2, 0],
                                w3[:, a * (C2 + 1) + 1:a * (C2 + 1) + C2]))

                        w0bT = wq[:, 0:CQ].unsqueeze(1).broadcast_to((128, LS, CQ))
                        w1bT = wq[:, CQ:2 * CQ].unsqueeze(1).broadcast_to((128, LS, CQ))
                        for a in range(2):
                            hv = n2[:].rearrange("p (s q) -> p s q", s=LS)
                            qv = sqt[:].rearrange("p (s q) -> p s q", s=LS)
                            s.op("vector", lambda e, a=a, mv=mv, hv=hv, w0bT=w0bT: e.tensor_tensor(
                                hv, mv[a * 2], w0bT, OP.mult))
                            s.op("vector", lambda e, a=a, mv=mv, qv=qv, w1bT=w1bT: e.tensor_tensor(
                                qv, mv[a * 2 + 1], w1bT, OP.mult))
                            vdrain()
                            s.op("vector", lambda e: e.tensor_tensor(n2[:], n2[:], sqt[:], OP.add))
                            vdrain()
                            s.op("vector", lambda e, a=a, jg=jg: e.tensor_copy(htile(hh, a, jg), n2[:]))
                            vdrain()
                            dcomb = s.op("vector", lambda e, a=a, jg=jg: e.tensor_tensor(
                                htile(hl, a, jg), n2[:], htile(hh, a, jg), OP.subtract), inc="sDVE")
                        prev_combine = dcomb
                    hs_done = prev_combine

                    if dbg and se == 0:
                        s.wait("gpsimd", [hs_done, s.val("sACT")])
                        s.op("gpsimd", lambda e: e.dma_start(out=dM, in_=Mt[:]), inc="sDBG", n=16)
                        s.op("gpsimd", lambda e: e.dma_start(out=dH, in_=hh[:]), inc="sDBG", n=16)

                    s.wait("tensor", [hs_done, w_ready])
                    s.wait("scalar", [hs_done])
                    aotP = [n2[:, 0:D], n2[:, D:2 * D]]
                    ln2tP = [sqt[:, 0:D], sqt[:, D:2 * D]]
                    prev_dao = [None, None]
                    prev_dln = [None, None]
                    prev_dpl = [None, None]
                    prev_dx2 = [None, None]
                    for tc in range(NTC):
                        par = tc % 2
                        idx = se * 16 + tc
                        W = par * 16
                        pi = 2 + (tc % 2)
                        aot = aotP[par]
                        ln2t = ln2tP[par]
                        s.wait("sync", [xq_cons[par]])
                        dep_x = s.op("sync", lambda e, se=se, tc=tc, par=par: e.dma_start(
                            out=xq[:, par * D:(par + 1) * D],
                            in_=xr[se, tc // 2, (tc % 2) * 128:(tc % 2) * 128 + 128, :]),
                            inc="sXQa" if par == 0 else "sXQb", n=16)
                        bank_acquire(2 * pi)
                        bank_acquire(2 * pi + 1)
                        dmm = None
                        for cc in range(4):
                            jgc, ac = cc // 2, cc % 2
                            for hs_ in (hh, hl):
                                wss = (wph, wpl) if hs_ is hh else (wph,)
                                for ws in wss:
                                    firstpass = (cc == 0) and (hs_ is hh) and (ws is wph)
                                    lastpass = (cc == 3) and (hs_ is hl)
                                    for half in range(2):
                                        lastmm = lastpass and (half == 1)
                                        dmm = s.op("tensor", lambda e, cc=cc, jgc=jgc, ac=ac, hs_=hs_, ws=ws, half=half, st=firstpass, sp=lastpass, pi=pi, tc=tc: e.matmul(
                                            PS[pi][:, half * 512:half * 512 + 512],
                                            htile(hs_, ac, jgc)[:, tc * 128:tc * 128 + 128],
                                            ws[:, (jgc * 2 + ac) * D + half * 512:(jgc * 2 + ac) * D + half * 512 + 512],
                                            start=st, stop=sp), inc="sPE" if lastmm else None)
                        bank_prod[2 * pi] = dmm
                        bank_prod[2 * pi + 1] = dmm
                        s.wait("scalar", [dep_x, prev_dao[par], prev_dx2[par]])
                        dy = s.op("scalar", lambda e, par=par, idx=idx, ln2t=ln2t: e.activation(
                            ln2t, xq[:, par * D:(par + 1) * D], AF.Identity,
                            scale=stripA[:, idx:idx + 1], bias=stripB[:, idx:idx + 1]), inc="sACT")
                        s.wait("vector", [dy, bank_prod[2 * pi], prev_dln[par], prev_dpl[par]])
                        dao = s.op("vector", lambda e, pi=pi, aot=aot, ln2t=ln2t: e.tensor_tensor(
                            aot, PS[pi][:], ln2t, OP.add), inc="sDVE")
                        bank_cons[2 * pi].append(dao)
                        bank_cons[2 * pi + 1].append(dao)
                        s.wait("scalar", [dao])
                        dvs, a2ap, b2ap = ln_strips(aot, W, idx, "plain")
                        s.wait("scalar", [dvs])
                        dln = s.op("scalar", lambda e, a2ap=a2ap, b2ap=b2ap, aot=aot, ln2t=ln2t: e.activation(
                            ln2t, aot, AF.Identity, scale=a2ap, bias=b2ap), inc="sACT")
                        s.wait("vector", [dln])
                        dx2 = s.op("vector", lambda e, par=par, aot=aot, ln2t=ln2t: e.tensor_tensor(
                            aot, xq[:, par * D:(par + 1) * D], ln2t, OP.add), inc="sDVE")
                        xq_cons[par] = dx2
                        s.wait("gpsimd", [dx2])
                        dst_row = se * T + tc * 128
                        dpl = s.op("gpsimd", lambda e, dst_row=dst_row, aot=aot: e.dma_start(
                            out=x2s[dst_row:dst_row + 128, :], in_=aot), inc="sX2W", n=16)
                        x2w[idx] = dpl
                        prev_dao[par], prev_dln[par], prev_dpl[par], prev_dx2[par] = dao, dln, dpl, dx2

        attn_pe = s.val("sPE")

        with ExitStack() as ffn:
            def fsb(name, shape, dt):
                return ffn.enter_context(nc.sbuf_tensor(name, shape, dt))
            h0h = fsb("h0h", [128, 32 * 512], BF16)
            h0l = fsb("h0l", [128, 32 * 512], BF16)
            h1h = fsb("h1h", [128, 32 * 512], BF16)
            h1l = fsb("h1l", [128, 32 * 512], BF16)
            xn3h = fsb("xn3h", [128, 8 * 512], BF16)
            xn3l = fsb("xn3l", [128, 8 * 512], BF16)
            w0b2 = fsb("w0b2", [128, 4 * 512], BF16)
            w1b2 = fsb("w1b2", [128, 4 * 512], BF16)
            w2b2 = fsb("w2b2", [128, 3 * 1024], BF16)
            x2t = fsb("x2t", [128, 4 * D], F32)
            xn3t = fsb("xn3t", [128, 4 * D], F32)
            r32 = fsb("r32", [128, 2 * 512], F32)
            lno = fsb("lno", [128, D], F32)
            ot = fsb("ot", [128, D], F32)

            FBAR()
            guardF = [s.val("sPE"), s.val("sACT"), s.val("sDVE")]
            s.wait("scalar", guardF)
            s.wait("vector", guardF)
            s.wait("scalar", [d_scs])
            s.wait("sync", conv_done + [attn_pe])
            s.wait("gpsimd", [attn_pe])
            x2t_cons = [None, None, None, None]
            xn3t_cons = [None, None, None, None]
            stw2 = fsb("stw2", [128, 32], F32)
            f_dvs = [None, None]
            f_dxn = [None, None]
            pend_front = {}
            pend_load = {}
            prev_dx3 = [None]
            wb_cons = {"w0": [None]*4, "w1": [None]*4, "w2": [None]*4}
            W0S_ = ("sW0a","sW0b","sW0c","sW0d"); W1S_ = ("sW1a","sW1b","sW1c","sW1d"); W2S_ = ("sW2a","sW2b","sW2c","sW2d")
            ot_cons = [None, None]
            r32_cons = [None, None]
            pend_fe = [None]

            for ft in range(NFT):
                se = ft // 4
                BAR("ft")
                def backF_evac(k):
                    base = (k % 2) * 2
                    src = PSF[:, base * 512:(base + 2) * 512].rearrange(
                        "p (b c t) -> p b c t", b=2, c=4)
                    dsth = xn3h[:].rearrange("p (b c t) -> p b c t", b=2, c=4)[:, :, :, k * 128:k * 128 + 128]
                    dstl = xn3l[:].rearrange("p (b c t) -> p b c t", b=2, c=4)[:, :, :, k * 128:k * 128 + 128]
                    s.wait("scalar", [bank_prod[base + 1]])
                    dhi = s.op("scalar", lambda e, dsth=dsth, src=src: e.copy(dsth, src), inc="sACT")
                    s.wait("vector", [dhi])
                    dlo = s.op("vector", lambda e, dstl=dstl, src=src, dsth=dsth: e.tensor_tensor(
                        dstl, src, dsth, OP.subtract), inc="sDVE")
                    bank_cons[base] = [dhi, dlo]
                    bank_cons[base + 1] = [dhi, dlo]
                    return dhi, dlo

                def loadF(fti, k):
                    gp = fti * 512 + k * 128
                    s.wait("gpsimd", [x2t_cons[k], x2w[gp // 128]])
                    dxl = s.op("gpsimd", lambda e, gp=gp, k=k: e.dma_start(
                        out=x2t[:, k * D:(k + 1) * D], in_=x2s[gp:gp + 128, :]),
                        inc="sX2a" if k % 2 == 0 else "sX2b", n=16)
                    pend_load[(fti, k)] = dxl

                def frontF(fti, k, SB, mid=None):
                    par = k % 2
                    dxl = pend_load.pop((fti, k))
                    s.wait("scalar", [dxl])
                    s.wait("vector", [f_dvs[par], f_dxn[par]])
                    dvs, a3ap, b3ap = ln_strips(x2t[:, k * D:(k + 1) * D], par * 16, 0, "plain",
                                                SB=SB, mid=mid)
                    f_dvs[par] = dvs
                    s.wait("scalar", [dvs, xn3t_cons[k]])
                    dxn = s.op("scalar", lambda e, k=k, a3ap=a3ap, b3ap=b3ap: e.activation(
                        xn3t[:, k * D:(k + 1) * D], x2t[:, k * D:(k + 1) * D],
                        AF.Identity, scale=a3ap, bias=b3ap), inc="sACT")
                    x2t_cons[k] = dxn
                    f_dxn[par] = dxn
                    pend_front[(fti, k)] = dxn

                def backF(fti, k):
                    dxn = pend_front.pop((fti, k))
                    s.wait("tensor", [dxn, d_id])
                    base = (k % 2) * 2
                    bank_acquire(base)
                    bank_acquire(base + 1)
                    dtr = None
                    for dc in range(8):
                        o = (base + dc // 4) * 512 + (dc % 4) * 128
                        dtr = s.op("tensor", lambda e, o=o, dc=dc, k=k: e.transpose(
                            PSF[:, o:o + 128], xn3t[:, k * D + dc * 128:k * D + (dc + 1) * 128], ident[:]), inc="sPE")
                    bank_prod[base] = dtr
                    bank_prod[base + 1] = dtr
                    xn3t_cons[k] = dtr

                if ft == 0:
                    for k in range(4):
                        loadF(0, k)
                    for k in range(4):
                        mid = (lambda k=k: backF_evac(k - 1)) if k > 0 else None
                        frontF(0, k, stw_g, mid=mid)
                        backF(0, k)
                    fe_hi, fe_lo = backF_evac(3)
                else:
                    fe_hi, fe_lo = pend_fe[0]

                if dbg and ft == 0:
                    s.wait("gpsimd", [s.val("sACT"), s.val("sDVE")])
                    s.op("gpsimd", lambda e: e.dma_start(out=dX3h, in_=xn3h[:]), inc="sDBG", n=16)
                    s.op("gpsimd", lambda e: e.dma_start(out=dX3l, in_=xn3l[:]), inc="sDBG", n=16)
                BAR("fg")
                xn3_done = [fe_hi, fe_lo]
                s.wait("tensor", xn3_done)
                s.wait("scalar", xn3_done)
                s.wait("vector", xn3_done)
                for grp in range(8):
                    b0 = (grp % 2) * 4
                    for b in range(4):
                        bank_acquire(b0 + b)
                    for dc in range(8):
                        par = (grp * 8 + dc) % 4
                        s.wait("sync", [wb_cons["w0"][par]])
                        dw = s.op("sync", lambda e, dc=dc, grp=grp, par=par: e.dma_start(
                            out=w0b2[:, par * 512:(par + 1) * 512],
                            in_=w0s[dc][:, grp * 512:(grp + 1) * 512]),
                            inc=W0S_[par], n=16)
                        s.wait("tensor", [dw])
                        for fi in range(4):
                            for xsp in (xn3h, xn3l):
                                lastbank = (dc == 7) and (xsp is xn3l)
                                lastbuf = (fi == 3) and (xsp is xn3l)
                                dmm = s.op("tensor", lambda e, dc=dc, fi=fi, par=par, xsp=xsp, st=(dc == 0 and xsp is xn3h), sp=lastbank, b0=b0: e.matmul(
                                    BANK(b0 + fi), w0b2[:, par * 512 + fi * 128:par * 512 + fi * 128 + 128],
                                    xsp[:, dc * 512:(dc + 1) * 512], start=st, stop=sp),
                                    inc="sPE" if (lastbank or lastbuf) else None)
                                if lastbank:
                                    bank_prod[b0 + fi] = dmm
                                if lastbuf:
                                    wb_cons["w0"][par] = dmm
                    for fi in range(4):
                        f0g = grp * 4 + fi
                        rpar = fi % 2
                        s.wait("scalar", [r32_cons[rpar]])
                        dr1 = bank_read("scalar", b0 + fi, lambda e, fi=fi, rpar=rpar, b0=b0: e.activation(
                            r32[:, rpar * 512:(rpar + 1) * 512], BANK(b0 + fi), AF.Relu, scale=scs[:, 3:4]), inc="sACT")
                        dr2 = s.op("scalar", lambda e, f0g=f0g, rpar=rpar: e.copy(
                            h0h[:, f0g * 512:(f0g + 1) * 512], r32[:, rpar * 512:(rpar + 1) * 512]), inc="sACT")
                        s.wait("vector", [dr2])
                        dlo = s.op("vector", lambda e, f0g=f0g, rpar=rpar: e.tensor_tensor(
                            h0l[:, f0g * 512:(f0g + 1) * 512], r32[:, rpar * 512:(rpar + 1) * 512],
                            h0h[:, f0g * 512:(f0g + 1) * 512], OP.subtract), inc="sDVE")
                        r32_cons[rpar] = dlo
                h0_done = s.val("sDVE")
                if dbg and ft == 0:
                    s.wait("gpsimd", [h0_done, s.val("sACT")])
                    s.op("gpsimd", lambda e: e.dma_start(out=dH0, in_=h0h[:]), inc="sDBG", n=16)

                BAR("gh")
                BARX("gh_ta", "tensor", ("sACT",))
                BARX("gh_sp", "sync", ("sPE", "sACT", "sDVE"))
                BARX("gh_v", "vector", ("sPE", "sACT"))
                BARX("gh_vp", "vector", ("sPE",))
                BARX("gh_va", "vector", ("sACT",))
                BARX("gh_a", "scalar", ("sPE", "sDVE"))
                BARX("gh_pl", "gpsimd", ("sPE", "sACT", "sDVE"))
                gh_done = [s.val("sPE"), s.val("sACT"), s.val("sDVE")]
                s.wait("tensor", gh_done)
                s.wait("scalar", gh_done)
                s.wait("vector", gh_done)
                for grp in range(8):
                    b0 = (grp % 2) * 4
                    for b in range(4):
                        bank_acquire(b0 + b)
                    for k in range(32):
                        par = (grp * 32 + k) % 4
                        s.wait("sync", [wb_cons["w1"][par]])
                        dw = s.op("sync", lambda e, k=k, grp=grp, par=par: e.dma_start(
                            out=w1b2[:, par * 512:(par + 1) * 512],
                            in_=w1s[k][:, grp * 512:(grp + 1) * 512]),
                            inc=W1S_[par], n=16)
                        s.wait("tensor", [dw])
                        for fi in range(4):
                            for xsp in (h0h, h0l):
                                lastbank = (k == 31) and (xsp is h0l)
                                lastbuf = (fi == 3) and (xsp is h0l)
                                dmm = s.op("tensor", lambda e, k=k, fi=fi, par=par, xsp=xsp, st=(k == 0 and xsp is h0h), sp=lastbank, b0=b0: e.matmul(
                                    BANK(b0 + fi), w1b2[:, par * 512 + fi * 128:par * 512 + fi * 128 + 128],
                                    xsp[:, k * 512:(k + 1) * 512], start=st, stop=sp),
                                    inc="sPE" if (lastbank or lastbuf) else None)
                                if lastbank:
                                    bank_prod[b0 + fi] = dmm
                                if lastbuf:
                                    wb_cons["w1"][par] = dmm
                    for fi in range(4):
                        f1g = grp * 4 + fi
                        rpar = fi % 2
                        s.wait("scalar", [r32_cons[rpar]])
                        dr1 = bank_read("scalar", b0 + fi, lambda e, fi=fi, rpar=rpar, b0=b0: e.activation(
                            r32[:, rpar * 512:(rpar + 1) * 512], BANK(b0 + fi), AF.Relu, scale=scs[:, 4:5]), inc="sACT")
                        dr2 = s.op("scalar", lambda e, f1g=f1g, rpar=rpar: e.copy(
                            h1h[:, f1g * 512:(f1g + 1) * 512], r32[:, rpar * 512:(rpar + 1) * 512]), inc="sACT")
                        s.wait("vector", [dr2])
                        dlo = s.op("vector", lambda e, f1g=f1g, rpar=rpar: e.tensor_tensor(
                            h1l[:, f1g * 512:(f1g + 1) * 512], r32[:, rpar * 512:(rpar + 1) * 512],
                            h1h[:, f1g * 512:(f1g + 1) * 512], OP.subtract), inc="sDVE")
                        r32_cons[rpar] = dlo
                h1_done = s.val("sDVE")
                if dbg and ft == 0:
                    s.wait("gpsimd", [h1_done, s.val("sACT")])
                    s.op("gpsimd", lambda e: e.dma_start(out=dH1, in_=h1h[:]), inc="sDBG", n=16)

                BAR("hi")
                hi_done = [s.val("sPE"), s.val("sACT"), s.val("sDVE")]
                s.wait("tensor", hi_done)
                s.wait("scalar", hi_done)
                s.wait("vector", hi_done)
                for b in range(8):
                    bank_acquire(b)

                def ImmGroup(tg):
                    dmm = None
                    for k in range(32):
                        par = (tg * 32 + k) % 3
                        s.wait("sync", [wb_cons["w2"][par]])
                        dw = s.op("sync", lambda e, k=k, par=par: e.dma_start(
                            out=w2b2[:, par * 1024:(par + 1) * 1024], in_=w2s[k]),
                            inc=W2S_[par], n=16)
                        s.wait("tensor", [dw])
                        for xsp in (h1h, h1l):
                            for tci in (2 * tg, 2 * tg + 1):
                                for half in range(2):
                                    lastbuf = (xsp is h1l) and (tci == 2 * tg + 1) and (half == 1)
                                    dmm = s.op("tensor", lambda e, k=k, par=par, xsp=xsp, tci=tci, half=half, st=(k == 0 and xsp is h1h), sp=(k == 31 and xsp is h1l): e.matmul(
                                        PS[tci][:, half * 512:half * 512 + 512],
                                        xsp[:, k * 512 + tci * 128:k * 512 + tci * 128 + 128],
                                        w2b2[:, par * 1024 + half * 512:par * 1024 + half * 512 + 512],
                                        start=st, stop=sp),
                                        inc="sPE" if lastbuf else None)
                                    if lastbuf:
                                        wb_cons["w2"][par] = dmm
                    for b in range(4 * tg, 4 * tg + 4):
                        bank_prod[b] = dmm

                dvsL = [None] * 4
                scbL = [None] * 4
                dlnL = [None] * 4

                def Istrips(tci):
                    par_ = tci % 2
                    s.wait("scalar", [bank_prod[2 * tci], dvsL[tci - 2] if tci >= 2 else None])
                    if tci >= 2:
                        s.wait("vector", [dlnL[tci - 2]])
                    dvs, sc4, b4 = ln_strips(PS[tci][:], par_ * 16, 0, "s2")
                    dvsL[tci], scbL[tci] = dvs, (sc4, b4)
                    bank_cons[2 * tci].append(dvs)
                    bank_cons[2 * tci + 1].append(dvs)

                def Ichain(tcis, strips_first=True):
                    if strips_first:
                        for tci in tcis:
                            Istrips(tci)
                    for tci in tcis:
                        par_ = tci % 2
                        gp = ft * 512 + tci * 128
                        sc4, b4 = scbL[tci]
                        s.wait("scalar", [dvsL[tci], prev_dx3[0]])
                        dln = s.op("scalar", lambda e, tci=tci, sc4=sc4, b4=b4: e.activation(
                            lno[:], PS[tci][:], AF.Identity, scale=sc4, bias=b4), inc="sACT")
                        bank_cons[2 * tci].append(dln)
                        bank_cons[2 * tci + 1].append(dln)
                        dlnL[tci] = dln
                        s.wait("gpsimd", [x2t_cons[tci]])
                        dxl = s.op("gpsimd", lambda e, gp=gp, tci=tci: e.dma_start(
                            out=x2t[:, tci * D:(tci + 1) * D], in_=x2s[gp:gp + 128, :]),
                            inc="sX2a" if tci % 2 == 0 else "sX2b", n=16)
                        s.wait("vector", [dln, dxl, ot_cons[0], ot_cons[1]])
                        dx3 = s.op("vector", lambda e, tci=tci: e.tensor_tensor(
                            ot[:], lno[:],
                            x2t[:, tci * D:(tci + 1) * D], OP.add), inc="sDVE")
                        x2t_cons[tci] = dx3
                        prev_dx3[0] = dx3
                        s.wait("gpsimd", [dx3])
                        pos = gp % T
                        dst = outr[se, pos // CQ, pos % CQ:pos % CQ + 128, :]
                        dout = s.op("gpsimd", lambda e, dst=dst: e.dma_start(
                            out=dst, in_=ot[:]),
                            inc="sOUTa" if par_ == 0 else "sOUTb", n=16)
                        ot_cons[par_] = dout
                        if ft + 1 < NFT:
                            loadF(ft + 1, tci)
                    if ft + 1 < NFT:
                        for tci in tcis:
                            frontF(ft + 1, tci, stw2)

                ImmGroup(0)
                Ichain((0, 1))
                ImmGroup(1)
                if ft + 1 < NFT:
                    backF(ft + 1, 0)
                    backF(ft + 1, 1)
                    Istrips(2)
                    Ichain((2,), strips_first=False)
                    backF_evac(0)
                    backF_evac(1)
                    backF(ft + 1, 2)
                    Istrips(3)
                    Ichain((3,), strips_first=False)
                    backF_evac(2)
                    backF(ft + 1, 3)
                    pend_fe[0] = backF_evac(3)
                else:
                    Ichain((2, 3))

            s.wait("gpsimd", [s.val(x) for x in ("sOUTa", "sOUTb", "sX2W", "sPSa", "sPSb", "sPSc", "sPSd", "sPSe", "sPSf", "sPL", "sDBG")])

        with nc.Block() as block:
            @block.sync
            def _(e):
                for fn in s.q["sync"]:
                    fn(e)

            @block.scalar
            def _(e):
                for fn in s.q["scalar"]:
                    fn(e)

            @block.vector
            def _(e):
                for fn in s.q["vector"]:
                    fn(e)

            @block.tensor
            def _(e):
                for fn in s.q["tensor"]:
                    fn(e)

            @block.gpsimd
            def _(e):
                for fn in s.q["gpsimd"]:
                    fn(e)
    return nc


def make_inputs(x_core, W_m, W_proj, v_vec, out_scale, k0, k1, k2, s0, s1, s2):
    f32 = np.float32
    Wm_r = np.asarray(W_m, f32).reshape(KJ, 4, D)
    wmt = np.ascontiguousarray(Wm_r.transpose(1, 0, 2).reshape(D, D).T)
    Wp = np.asarray(W_proj, f32)
    idxs = np.empty(512, np.int64)
    for jg in range(2):
        for a in range(2):
            jj = np.arange(128)
            idxs[jg * 256 + a * 128 + jj] = (jg * 128 + jj) * 2 + a
    wpt = np.ascontiguousarray(Wp.T[idxs, :])
    vvh = np.ascontiguousarray(np.asarray(v_vec, f32).reshape(KJ, 2))
    osr = np.ascontiguousarray(np.broadcast_to(np.asarray(out_scale, f32), (128, D)))
    idin = np.eye(128, dtype=f32)
    k0u = np.asarray(k0).astype(np.uint8)
    k0t_, k0b_ = np.ascontiguousarray(k0u[:D]), np.ascontiguousarray(k0u[D:])
    k1u_ = np.ascontiguousarray(np.asarray(k1).astype(np.uint8))
    k2u_ = np.ascontiguousarray(np.asarray(k2).astype(np.uint8))
    scin = np.ascontiguousarray(np.broadcast_to(np.stack(
        [np.float32(s0), np.float32(s1), np.float32(s2)]), (128, 3)))
    return {
        "xin": np.ascontiguousarray(np.asarray(x_core, f32).reshape(NPOS, D)),
        "wmt": wmt, "wpt": wpt, "vv": vvh, "osr": osr, "idin": idin,
        "k0t": k0t_, "k0b": k0b_, "k1u": k1u_, "k2u": k2u_, "scin": scin,
    }


def kernel(L0_attn_x, L0_attn_W_m, L0_attn_v_vec, L0_attn_W_proj, L0_attn_out_scale,
           L0_ffn_k0, L0_ffn_k1, L0_ffn_k2, L0_ffn_s0, L0_ffn_s1, L0_ffn_s2):
    x = np.asarray(L0_attn_x, np.float32)
    nc = build(dbg=False)
    in_maps = []
    for c in range(8):
        in_maps.append(make_inputs(
            x[2 * c:2 * c + 2], L0_attn_W_m, L0_attn_W_proj, L0_attn_v_vec,
            L0_attn_out_scale, L0_ffn_k0, L0_ffn_k1, L0_ffn_k2,
            L0_ffn_s0, L0_ffn_s1, L0_ffn_s2))
    res = run_bass_kernel_spmd(nc, in_maps, core_ids=list(range(8)))
    outs = [res.results[c]["out"].reshape(SEQ, T, D) for c in range(8)]
    return np.ascontiguousarray(np.concatenate(outs, axis=0))


# version 33
# speedup vs baseline: 1.0278x; 1.0278x over previous
import numpy as np
import ml_dtypes
from contextlib import ExitStack

import concourse.bass as bass
import concourse.mybir as mybir
from concourse.bass_utils import run_bass_kernel_spmd
from concourse.alu_op_type import AluOpType as OP

F32 = mybir.dt.float32
BF16 = mybir.dt.bfloat16
U8 = mybir.dt.uint8
AF = mybir.ActivationFunctionType

SEQ = 2
T = 2048
D = 1024
NPOS = SEQ * T
KJ = 256
F0 = 4096
F1 = 4096
LS = 8
CQ = 256
L2S = 8
C2 = 32
NTC = T // 128
NFT = NPOS // 512
EPS_LN = 1e-5
EPS_RMS = 1.1920929e-07
RD = 1.0 / D


class Sched:
    ENGS = ("sync", "scalar", "vector", "tensor", "gpsimd")

    def __init__(self):
        self.q = {e: [] for e in self.ENGS}
        self.cnt = {}
        self.waited = {}

    def bind(self, sems):
        self.sems = sems

    def wait(self, eng, dep):
        if dep is None:
            return
        deps = dep if isinstance(dep, list) else [dep]
        for d in deps:
            if d is None:
                continue
            semname, val = d
            if val <= 0:
                continue
            key = (eng, semname)
            if self.waited.get(key, 0) >= val:
                continue
            self.waited[key] = val
            h = self.sems[semname]
            self.q[eng].append(lambda e, h=h, v=val: e.wait_ge(h, v))

    def op(self, eng, fn, inc=None, n=1):
        if inc is None:
            self.q[eng].append(fn)
            return None
        self.cnt[inc] = self.cnt.get(inc, 0) + n
        newval = self.cnt[inc]

        def g(e, fn=fn, inc=inc, n=n):
            fn(e).then_inc(self.sems[inc], n)
        self.q[eng].append(g)
        return (inc, newval)

    def val(self, sem):
        return (sem, self.cnt.get(sem, 0))


def build(dbg=False, barriers=()):
    nc = bass.Bass("TRN2", target_bir_lowering=False)

    xin = nc.dram_tensor("xin", [NPOS, D], F32, kind="ExternalInput").ap()
    wmt = nc.dram_tensor("wmt", [D, D], F32, kind="ExternalInput").ap()
    wpt = nc.dram_tensor("wpt", [512, D], F32, kind="ExternalInput").ap()
    vv = nc.dram_tensor("vv", [KJ, 2], F32, kind="ExternalInput").ap()
    osr = nc.dram_tensor("osr", [128, D], F32, kind="ExternalInput").ap()
    idin = nc.dram_tensor("idin", [128, 128], F32, kind="ExternalInput").ap()
    k0t = nc.dram_tensor("k0t", [D, F0], U8, kind="ExternalInput").ap()
    k0b = nc.dram_tensor("k0b", [D, F0], U8, kind="ExternalInput").ap()
    k1u = nc.dram_tensor("k1u", [F0, F1], U8, kind="ExternalInput").ap()
    k2u = nc.dram_tensor("k2u", [F1, D], U8, kind="ExternalInput").ap()
    scin = nc.dram_tensor("scin", [128, 3], F32, kind="ExternalInput").ap()
    out = nc.dram_tensor("out", [NPOS, D], F32, kind="ExternalOutput").ap()

    x2s = nc.dram_tensor("x2s", [NPOS, D], F32,
                         kind="ExternalOutput" if dbg else "Internal").ap()
    wkind = "ExternalOutput" if dbg else "Internal"
    w0s = nc.dram_tensor("w0s", [8, 128, F0], BF16, kind=wkind).ap()
    w1s = nc.dram_tensor("w1s", [32, 128, F1], BF16, kind=wkind).ap()
    w2s = nc.dram_tensor("w2s", [32, 128, D], BF16, kind=wkind).ap()
    dM = dH = dXN = dMP = dH0 = dH1 = None
    if dbg:
        dM = nc.dram_tensor("dM", [128, 8 * T], F32, kind="ExternalOutput").ap()
        dH = nc.dram_tensor("dH", [128, 4 * T], F32, kind="ExternalOutput").ap()
        dXN = nc.dram_tensor("dXN", [128, 8 * T], BF16, kind="ExternalOutput").ap()
        dMP = nc.dram_tensor("dMP", [128, 8 * T], F32, kind="ExternalOutput").ap()
        dH0 = nc.dram_tensor("dH0", [128, 32 * 512], BF16, kind="ExternalOutput").ap()
        dH1 = nc.dram_tensor("dH1", [128, 32 * 512], BF16, kind="ExternalOutput").ap()
        dX3h = nc.dram_tensor("dX3h", [128, 8 * 512], BF16, kind="ExternalOutput").ap()
        dX3l = nc.dram_tensor("dX3l", [128, 8 * 512], BF16, kind="ExternalOutput").ap()

    xr = xin.rearrange("(se q s) d -> se s q d", se=SEQ, s=LS)
    outr = out.rearrange("(se q s) d -> se s q d", se=SEQ, s=LS)

    s = Sched()
    _uid = [0]

    def uname(base):
        _uid[0] += 1
        return f"{base}_{_uid[0]}"

    with ExitStack() as ctx:
        def sb(name, shape, dt):
            return ctx.enter_context(nc.sbuf_tensor(name, shape, dt))

        PSF = ctx.enter_context(nc.psum_tensor("psf", [128, 4096], F32))
        PS = [PSF[:, i * 1024:(i + 1) * 1024] for i in range(4)]

        def BANK(k):
            return PSF[:, k * 512:(k + 1) * 512]

        def BANKPE(k):
            return PSF[:, k * 512:k * 512 + 128]

        def PSALL():
            return PSF[:].rearrange("p (b c) -> p b c", b=8)[:, :, 0:128]

        SEMNAMES = ("sLD", "sPL", "sPE", "sACT", "sDVE", "sCST", "sPKa", "sPKb",
                    "sPSa", "sPSb", "sWT", "sXQa", "sXQb", "sX2W", "sW0a", "sW0b",
                    "sW1a", "sW1b", "sW2a", "sW2b", "sX2a", "sX2b", "sOUTa", "sOUTb",
                    "sDBG", "sW0c", "sW0d", "sW1c", "sW1d", "sW2c", "sW2d",
                    "sPKc", "sPKd", "sPSc", "sPSd", "sPKe", "sPKf", "sPSe", "sPSf")
        sems = {n: ctx.enter_context(nc.semaphore(n)) for n in SEMNAMES}
        s.bind(sems)

        bank_prod = [None] * 8
        bank_cons = [[] for _ in range(8)]

        def bank_acquire(k):
            s.wait("tensor", bank_cons[k])
            bank_cons[k] = []

        def bank_read(eng, k, fn, inc):
            s.wait(eng, [bank_prod[k]])
            dep = s.op(eng, fn, inc=inc)
            bank_cons[k].append(dep)
            return dep

        ident = sb("ident", [128, 128], F32)
        vt = sb("vt", [128, 4], F32)
        scs = sb("scs", [128, 8], F32)
        stripA = sb("stripA", [128, 32], F32)
        stripB = sb("stripB", [128, 32], F32)
        stw_g = sb("stw", [128, 32], F32)
        dum = sb("dum", [128, D], F32)

        d_id = s.op("sync", lambda e: e.dma_start(out=ident[:], in_=idin), inc="sCST", n=16)
        d_sc = s.op("sync", lambda e: e.dma_start(out=scs[:, 0:3], in_=scin), inc="sCST", n=16)
        d_vt = []
        for a in range(2):
            for jg in range(2):
                d_vt.append(s.op("sync", lambda e, a=a, jg=jg: e.dma_start(
                    out=vt[:, a * 2 + jg:a * 2 + jg + 1],
                    in_=vv[jg * 128:(jg + 1) * 128, a:a + 1],
                    allow_slow_non_contiguous=True), inc="sCST", n=16))
        cst_all = ("sCST", 6 * 16)
        s.wait("vector", [cst_all])
        s.op("vector", lambda e: e.tensor_scalar(scs[:, 3:4], scs[:, 0:1], 4.0, None, OP.mult))
        s.op("vector", lambda e: e.tensor_scalar(scs[:, 4:5], scs[:, 1:2], 2.0, None, OP.mult))
        d_scs = s.op("vector", lambda e: e.tensor_tensor(scs[:, 5:6], scs[:, 2:3], scs[:, 2:3], OP.mult), inc="sDVE")

        def vdrain():
            s.op("vector", lambda e: e.drain())

        def BAR(tag):
            if tag not in barriers:
                return
            deps = [s.val(x) for x in SEMNAMES]
            for eng in ("scalar", "vector", "tensor", "gpsimd", "sync"):
                s.wait(eng, deps)

        def BARX(tag, eng, semnames):
            if tag not in barriers:
                return
            s.wait(eng, [s.val(x) for x in semnames])

        def FBAR():
            deps = [s.val(x) for x in SEMNAMES]
            for eng in ("scalar", "vector", "tensor", "gpsimd", "sync"):
                s.wait(eng, deps)

        def ln_strips(src_ap, W, idx, mode, SB=None, mid=None):
            stw = SB if SB is not None else stw_g
            a1_ = s.op("scalar", lambda e, stw=stw: e.activation(
                dum[:], src_ap, AF.Copy, accum_out=stw[:, W + 0:W + 1]), inc="sACT")
            a2_ = s.op("scalar", lambda e, stw=stw: e.activation(
                dum[:], src_ap, AF.Square, accum_out=stw[:, W + 1:W + 2]), inc="sACT")
            if mid is not None:
                mid()
            s.wait("vector", [a1_, a2_])
            s.op("vector", lambda e: e.tensor_scalar(stw[:, W + 2:W + 3], stw[:, W + 0:W + 1], RD, None, OP.mult))
            s.op("vector", lambda e: e.tensor_scalar(stw[:, W + 3:W + 4], stw[:, W + 1:W + 2], RD, None, OP.mult))
            s.op("vector", lambda e: e.tensor_tensor(stw[:, W + 4:W + 5], stw[:, W + 2:W + 3], stw[:, W + 2:W + 3], OP.mult))
            vdrain()
            s.op("vector", lambda e: e.tensor_tensor(stw[:, W + 5:W + 6], stw[:, W + 3:W + 4], stw[:, W + 4:W + 5], OP.subtract))
            vdrain()
            if mode == "s2":
                s.wait("vector", [d_scs])
                dv1 = s.op("vector", lambda e: e.tensor_scalar(
                    stw[:, W + 6:W + 7], stw[:, W + 5:W + 6], scs[:, 5:6], EPS_LN, OP.mult, OP.add), inc="sDVE")
                s.wait("scalar", [dv1])
                da1 = s.op("scalar", lambda e: e.activation(stw[:, W + 7:W + 8], stw[:, W + 6:W + 7], AF.Sqrt), inc="sACT")
                s.wait("vector", [da1])
                s.op("vector", lambda e: e.reciprocal(stw[:, W + 8:W + 9], stw[:, W + 7:W + 8]))
                vdrain()
                s.op("vector", lambda e: e.tensor_tensor(stw[:, W + 8:W + 9], stw[:, W + 8:W + 9], scs[:, 2:3], OP.mult))
                vdrain()
                dv3 = s.op("vector", lambda e: e.scalar_tensor_tensor(
                    stw[:, W + 9:W + 10], stw[:, W + 2:W + 3], -1.0, stw[:, W + 8:W + 9], OP.mult, OP.mult), inc="sDVE")
                return dv3, stw[:, W + 8:W + 9], stw[:, W + 9:W + 10]
            if mode == "plain":
                dv1 = s.op("vector", lambda e: e.tensor_scalar(
                    stw[:, W + 6:W + 7], stw[:, W + 5:W + 6], EPS_LN, None, OP.add), inc="sDVE")
                s.wait("scalar", [dv1])
                da1 = s.op("scalar", lambda e: e.activation(stw[:, W + 7:W + 8], stw[:, W + 6:W + 7], AF.Sqrt), inc="sACT")
                s.wait("vector", [da1])
                s.op("vector", lambda e: e.reciprocal(stw[:, W + 8:W + 9], stw[:, W + 7:W + 8]))
                vdrain()
                dv3 = s.op("vector", lambda e: e.scalar_tensor_tensor(
                    stw[:, W + 9:W + 10], stw[:, W + 2:W + 3], -1.0, stw[:, W + 8:W + 9], OP.mult, OP.mult), inc="sDVE")
                return dv3, stw[:, W + 8:W + 9], stw[:, W + 9:W + 10]
            s.op("vector", lambda e: e.tensor_scalar(stw[:, W + 6:W + 7], stw[:, W + 5:W + 6], EPS_LN, None, OP.add))
            vdrain()
            s.op("vector", lambda e: e.reciprocal(stw[:, W + 8:W + 9], stw[:, W + 6:W + 7]))
            vdrain()
            s.op("vector", lambda e: e.tensor_tensor(stw[:, W + 9:W + 10], stw[:, W + 5:W + 6], stw[:, W + 8:W + 9], OP.mult))
            vdrain()
            dv1 = s.op("vector", lambda e: e.tensor_scalar(stw[:, W + 10:W + 11], stw[:, W + 9:W + 10], EPS_RMS, None, OP.add), inc="sDVE")
            s.wait("scalar", [dv1])
            s.op("scalar", lambda e: e.activation(stw[:, W + 7:W + 8], stw[:, W + 6:W + 7], AF.Sqrt))
            da1 = s.op("scalar", lambda e: e.activation(stw[:, W + 11:W + 12], stw[:, W + 10:W + 11], AF.Sqrt), inc="sACT")
            s.wait("vector", [da1])
            s.op("vector", lambda e, idx=idx: e.reciprocal(stripA[:, idx:idx + 1], stw[:, W + 7:W + 8]))
            s.op("vector", lambda e: e.reciprocal(stw[:, W + 12:W + 13], stw[:, W + 11:W + 12]))
            vdrain()
            s.op("vector", lambda e, idx=idx: e.tensor_tensor(stw[:, W + 13:W + 14], stripA[:, idx:idx + 1], stw[:, W + 12:W + 13], OP.mult))
            s.op("vector", lambda e, idx=idx: e.scalar_tensor_tensor(
                stripB[:, idx:idx + 1], stw[:, W + 2:W + 3], -1.0, stripA[:, idx:idx + 1], OP.mult, OP.mult))
            vdrain()
            dv3 = s.op("vector", lambda e: e.scalar_tensor_tensor(
                stw[:, W + 14:W + 15], stw[:, W + 2:W + 3], -1.0, stw[:, W + 13:W + 14], OP.mult, OP.mult), inc="sDVE")
            return dv3, stw[:, W + 13:W + 14], stw[:, W + 14:W + 15]

        with ExitStack() as prep:
            u8a = prep.enter_context(nc.sbuf_tensor("u8a", [128, 6 * F0], U8))
            u8b = prep.enter_context(nc.sbuf_tensor("u8b", [128, 2 * F0], U8))
            wcv = prep.enter_context(nc.sbuf_tensor("wcv", [128, 6 * F0], BF16))
            pl_by_par = [None] * 6
            PKS_ = ("sPKa", "sPKb", "sPKc", "sPKd", "sPKe", "sPKf")
            PSS_ = ("sPSa", "sPSb", "sPSc", "sPSd", "sPSe", "sPSf")
            for dc in range(8):
                p_ = dc % 2
                s.wait("sync", [pl_by_par[p_]])
                da = s.op("sync", lambda e, dc=dc, p_=p_: e.dma_start(
                    out=u8a[:, p_ * F0:(p_ + 1) * F0], in_=k0t[dc * 128:(dc + 1) * 128, :]),
                    inc=PKS_[p_], n=16)
                db = s.op("sync", lambda e, dc=dc, p_=p_: e.dma_start(
                    out=u8b[:, p_ * F0:(p_ + 1) * F0], in_=k0b[dc * 128:(dc + 1) * 128, :]),
                    inc=PKS_[p_], n=16)
                s.wait("vector", [da, db, pl_by_par[p_]])
                dv = s.op("vector", lambda e, p_=p_: e.tensor_tensor(
                    wcv[:, p_ * F0:(p_ + 1) * F0], u8a[:, p_ * F0:(p_ + 1) * F0],
                    u8b[:, p_ * F0:(p_ + 1) * F0], OP.subtract), inc="sDVE")
                q_ = "gpsimd" if p_ < 1 else "scalar"
                s.wait(q_, [dv])
                pl_by_par[p_] = s.op(q_, lambda e, dc=dc, p_=p_: e.dma_start(
                    out=w0s[dc], in_=wcv[:, p_ * F0:(p_ + 1) * F0]),
                    inc=PSS_[p_], n=16)
            for k in range(32):
                p_ = k % 6
                s.wait("sync", [pl_by_par[p_]])
                dl = s.op("sync", lambda e, k=k, p_=p_: e.dma_start(
                    out=u8a[:, p_ * F0:(p_ + 1) * F0], in_=k1u[k * 128:(k + 1) * 128, :]),
                    inc=PKS_[p_], n=16)
                s.wait("scalar", [dl, pl_by_par[p_]])
                dc_ = s.op("scalar", lambda e, p_=p_: e.activation(
                    wcv[:, p_ * F0:(p_ + 1) * F0], u8a[:, p_ * F0:(p_ + 1) * F0],
                    AF.Copy, bias=-1.0, scale=2.0), inc="sACT")
                q_ = "gpsimd" if p_ < 3 else "scalar"
                s.wait(q_, [dc_])
                pl_by_par[p_] = s.op(q_, lambda e, k=k, p_=p_: e.dma_start(
                    out=w1s[k], in_=wcv[:, p_ * F0:(p_ + 1) * F0]),
                    inc=PSS_[p_], n=16)
            for k in range(32):
                p_ = k % 6
                s.wait("sync", [pl_by_par[p_]])
                dl = s.op("sync", lambda e, k=k, p_=p_: e.dma_start(
                    out=u8a[:, p_ * F0:p_ * F0 + D], in_=k2u[k * 128:(k + 1) * 128, :]),
                    inc=PKS_[p_], n=16)
                s.wait("scalar", [dl, pl_by_par[p_]])
                dc_ = s.op("scalar", lambda e, p_=p_: e.activation(
                    wcv[:, p_ * F0:p_ * F0 + D], u8a[:, p_ * F0:p_ * F0 + D],
                    AF.Copy, bias=-1.0, scale=2.0), inc="sACT")
                q_ = "gpsimd" if p_ < 3 else "scalar"
                s.wait(q_, [dc_])
                pl_by_par[p_] = s.op(q_, lambda e, k=k, p_=p_: e.dma_start(
                    out=w2s[k], in_=wcv[:, p_ * F0:p_ * F0 + D]),
                    inc=PSS_[p_], n=16)
        conv_done = [s.val(x) for x in PSS_]
        FBAR()

        x2w = [None] * 32
        xq_cons = [None, None]

        with ExitStack() as attn:
            def asb(name, shape, dt):
                return attn.enter_context(nc.sbuf_tensor(name, shape, dt))
            wmh = asb("wmh", [128, 8 * D], BF16)
            wml = asb("wml", [128, 8 * D], BF16)
            wph = asb("wph", [128, 4 * D], BF16)
            wpl = asb("wpl", [128, 4 * D], BF16)
            xq = asb("xq", [128, 2 * D], F32)
            xnt = asb("xnt", [128, 2 * D], F32)
            Mt = asb("Mt", [128, 8 * T], F32)

            def Mtile(comp, jg):
                return Mt[:, (comp * 2 + jg) * T:(comp * 2 + jg + 1) * T]

            with ExitStack() as wprep:
                wtmp = wprep.enter_context(nc.sbuf_tensor("wtmp", [128, D], F32))
                ostmp = wprep.enter_context(nc.sbuf_tensor("ostmp", [128, D], F32))
                d_os = s.op("sync", lambda e: e.dma_start(out=ostmp[:], in_=osr), inc="sWT", n=16)
                last = None
                for dc in range(8):
                    s.wait("sync", [last])
                    dl = s.op("sync", lambda e, dc=dc: e.dma_start(
                        out=wtmp[:], in_=wmt[dc * 128:(dc + 1) * 128, :]), inc="sWT", n=16)
                    s.wait("vector", [dl])
                    s.op("vector", lambda e, dc=dc: e.tensor_copy(wmh[:, dc * D:(dc + 1) * D], wtmp[:]))
                    s.op("vector", lambda e: e.drain())
                    last = s.op("vector", lambda e, dc=dc: e.tensor_tensor(
                        wml[:, dc * D:(dc + 1) * D], wtmp[:], wmh[:, dc * D:(dc + 1) * D], OP.subtract), inc="sDVE")
                for cc in range(4):
                    s.wait("sync", [last])
                    dl = s.op("sync", lambda e, cc=cc: e.dma_start(
                        out=wtmp[:], in_=wpt[cc * 128:(cc + 1) * 128, :]), inc="sWT", n=16)
                    s.wait("vector", [dl, d_os])
                    s.op("vector", lambda e: e.tensor_tensor(wtmp[:], wtmp[:], ostmp[:], OP.mult))
                    s.op("vector", lambda e: e.drain())
                    s.op("vector", lambda e, cc=cc: e.tensor_copy(wph[:, cc * D:(cc + 1) * D], wtmp[:]))
                    s.op("vector", lambda e: e.drain())
                    last = s.op("vector", lambda e, cc=cc: e.tensor_tensor(
                        wpl[:, cc * D:(cc + 1) * D], wtmp[:], wph[:, cc * D:(cc + 1) * D], OP.subtract), inc="sDVE")
            w_ready = last

            xnt_cons = [None, None]
            for se in range(SEQ):
                with ExitStack() as sub1:
                    xnh = sub1.enter_context(nc.sbuf_tensor(uname("xnh"), [128, 8 * T], BF16))
                    xnl = sub1.enter_context(nc.sbuf_tensor(uname("xnl"), [128, 8 * T], BF16))
                    FBAR()
                    guard = [s.val("sPE"), s.val("sACT"), s.val("sDVE")]

                    def backA_evac(tc):
                        base = (tc % 2) * 2
                        src = PSF[:, base * 512:(base + 2) * 512].rearrange(
                            "p (b c t) -> p b c t", b=2, c=4)
                        dsth = xnh[:].rearrange("p (b c t) -> p b c t", b=2, c=4)[:, :, :, tc * 128:tc * 128 + 128]
                        dstl = xnl[:].rearrange("p (b c t) -> p b c t", b=2, c=4)[:, :, :, tc * 128:tc * 128 + 128]
                        s.wait("scalar", [bank_prod[base + 1]])
                        dhi = s.op("scalar", lambda e, dsth=dsth, src=src: e.copy(dsth, src), inc="sACT")
                        s.wait("vector", [dhi])
                        dlo = s.op("vector", lambda e, dstl=dstl, src=src, dsth=dsth: e.tensor_tensor(
                            dstl, src, dsth, OP.subtract), inc="sDVE")
                        bank_cons[base] = [dhi, dlo]
                        bank_cons[base + 1] = [dhi, dlo]

                    for tc in range(NTC):
                        par = tc % 2
                        idx = se * 16 + tc
                        W = par * 16
                        s.wait("sync", [xq_cons[par]])
                        dep_x = s.op("sync", lambda e, se=se, tc=tc, par=par: e.dma_start(
                            out=xq[:, par * D:(par + 1) * D],
                            in_=xr[se, tc // 2, (tc % 2) * 128:(tc % 2) * 128 + 128, :]),
                            inc="sXQa" if par == 0 else "sXQb", n=16)
                        s.wait("scalar", [dep_x])
                        if tc == 0:
                            s.wait("scalar", guard)
                            s.wait("vector", guard)
                        mid = (lambda tc=tc: backA_evac(tc - 1)) if tc > 0 else None
                        dv3, sc_ap, bi_ap = ln_strips(xq[:, par * D:(par + 1) * D], W, idx, "rms", mid=mid)
                        s.wait("scalar", [dv3, xnt_cons[par]])
                        dxn = s.op("scalar", lambda e, par=par, sc_ap=sc_ap, bi_ap=bi_ap: e.activation(
                            xnt[:, par * D:(par + 1) * D], xq[:, par * D:(par + 1) * D],
                            AF.Identity, scale=sc_ap, bias=bi_ap), inc="sACT")
                        xq_cons[par] = dxn
                        s.wait("tensor", [dxn, d_id])
                        base = (tc % 2) * 2
                        bank_acquire(base)
                        bank_acquire(base + 1)
                        dtr = None
                        for dc in range(8):
                            o = (base + dc // 4) * 512 + (dc % 4) * 128
                            dtr = s.op("tensor", lambda e, o=o, dc=dc, par=par: e.transpose(
                                PSF[:, o:o + 128], xnt[:, par * D + dc * 128:par * D + (dc + 1) * 128], ident[:]), inc="sPE")
                        bank_prod[base] = dtr
                        bank_prod[base + 1] = dtr
                        xnt_cons[par] = dtr
                    backA_evac(NTC - 1)

                    def head_norm_l1():
                        ms = [Mtile(c, 0) for c in range(4)]
                        mv = [m.rearrange("p (s q) -> p s q", s=LS) for m in ms]
                        HD = T // 2
                        nA = xnt[:, 0:HD]
                        nB = xnt[:, HD:2 * HD]
                        dmult = None
                        for hf in range(2):
                            hs_ = hf * HD
                            s.wait("scalar", [dmult])
                            dn = s.op("scalar", lambda e, ms=ms, hs_=hs_: e.activation(
                                nA, ms[0][:, hs_:hs_ + HD], AF.Square), inc="sACT")
                            dadd = None
                            for c in (1, 2, 3):
                                s.wait("scalar", [dadd])
                                dq = s.op("scalar", lambda e, ms=ms, c=c, hs_=hs_: e.activation(
                                    nB, ms[c][:, hs_:hs_ + HD], AF.Square), inc="sACT")
                                s.wait("vector", [dn, dq])
                                vdrain()
                                dadd = s.op("vector", lambda e: e.tensor_tensor(nA, nA, nB, OP.add), inc="sDVE")
                            vdrain()
                            dnv = s.op("vector", lambda e: e.tensor_scalar(nA, nA, 1e-12, None, OP.max), inc="sDVE")
                            s.wait("scalar", [dnv])
                            s.op("scalar", lambda e: e.activation(nA, nA, AF.Ln))
                            dr = s.op("scalar", lambda e: e.activation(nA, nA, AF.Exp, scale=-0.5), inc="sACT")
                            s.wait("vector", [dr])
                            for c in range(4):
                                dmult = s.op("vector", lambda e, ms=ms, c=c, hs_=hs_: e.tensor_tensor(
                                    ms[c][:, hs_:hs_ + HD], ms[c][:, hs_:hs_ + HD], nA, OP.mult),
                                    inc="sDVE" if c == 3 else None)
                        vdrain()
                        tq_ = [xnt[:, i * CQ:(i + 1) * CQ] for i in range(4)]
                        for st_ in range(1, LS):
                            cur = [v[:, st_, :] for v in mv]
                            prv = [v[:, st_ - 1, :] for v in mv]
                            s.op("vector", lambda e, cur=cur, prv=prv: e.tensor_tensor(tq_[0], cur[0], prv[0], OP.mult))
                            s.op("vector", lambda e, cur=cur, prv=prv: e.tensor_tensor(tq_[1], cur[1], prv[2], OP.mult))
                            s.op("vector", lambda e, cur=cur, prv=prv: e.tensor_tensor(tq_[2], cur[0], prv[1], OP.mult))
                            s.op("vector", lambda e, cur=cur, prv=prv: e.tensor_tensor(tq_[3], cur[1], prv[3], OP.mult))
                            s.op("vector", lambda e, cur=cur: e.tensor_tensor(cur[0], tq_[0], tq_[1], OP.add))
                            s.op("vector", lambda e, cur=cur: e.tensor_tensor(cur[1], tq_[2], tq_[3], OP.add))
                            s.op("vector", lambda e, cur=cur, prv=prv: e.tensor_tensor(tq_[0], cur[2], prv[0], OP.mult))
                            s.op("vector", lambda e, cur=cur, prv=prv: e.tensor_tensor(tq_[1], cur[3], prv[2], OP.mult))
                            s.op("vector", lambda e, cur=cur, prv=prv: e.tensor_tensor(tq_[2], cur[2], prv[1], OP.mult))
                            s.op("vector", lambda e, cur=cur, prv=prv: e.tensor_tensor(tq_[3], cur[3], prv[3], OP.mult))
                            s.op("vector", lambda e, cur=cur: e.tensor_tensor(cur[2], tq_[0], tq_[1], OP.add))
                            s.op("vector", lambda e, cur=cur: e.tensor_tensor(cur[3], tq_[2], tq_[3], OP.add))
                        return s.op("vector", lambda e: e.drain(), inc="sDVE")

                    s.wait("tensor", [w_ready, s.val("sACT"), s.val("sDVE")])
                    for fc in (0, 2, 4, 6, 1, 3, 5, 7):
                        grp = (fc % 2) * 4
                        for tt in range(4):
                            bank_acquire(grp + tt)
                        for dc in range(8):
                            for (wsp, xsp) in ((wmh, xnh), (wmh, xnl), (wml, xnh)):
                                firstpass = (dc == 0) and (wsp is wmh) and (xsp is xnh)
                                lastpass = (dc == 7) and (wsp is wml)
                                for tt in range(4):
                                    dmm = s.op("tensor", lambda e, fc=fc, dc=dc, wsp=wsp, xsp=xsp, tt=tt, st=firstpass, sp=lastpass, grp=grp: e.matmul(
                                        BANK(grp + tt),
                                        wsp[:, dc * D + fc * 128:dc * D + fc * 128 + 128],
                                        xsp[:, dc * T + tt * 512:dc * T + (tt + 1) * 512],
                                        start=st, stop=sp), inc="sPE" if lastpass else None)
                                    if lastpass:
                                        bank_prod[grp + tt] = dmm
                        comp, jg = fc // 2, fc % 2
                        for tt in range(4):
                            bank_read("scalar", grp + tt, lambda e, comp=comp, jg=jg, tt=tt, grp=grp: e.copy(
                                Mtile(comp, jg)[:, tt * 512:(tt + 1) * 512], BANK(grp + tt)), inc="sACT")
                        if fc == 6:
                            jg0_ready = [s.val("sPE"), s.val("sACT")]
                            s.wait("scalar", jg0_ready)
                            s.wait("vector", jg0_ready)
                            d_scan0 = head_norm_l1()
                            xnt_cons[0] = d_scan0
                            xnt_cons[1] = d_scan0
                    if dbg and se == 0:
                        s.wait("gpsimd", [s.val("sACT"), s.val("sDVE")])
                        s.op("gpsimd", lambda e: e.dma_start(out=dXN, in_=xnh[:]), inc="sDBG", n=16)
                        s.op("gpsimd", lambda e: e.dma_start(out=dMP, in_=Mt[:]), inc="sDBG", n=16)
                b_done = [s.val("sPE"), s.val("sACT")]

                with ExitStack() as sub2:
                    def s2b(name, shape, dt):
                        return sub2.enter_context(nc.sbuf_tensor(uname(name), shape, dt))
                    t1 = s2b("t1", [128, CQ], F32)
                    t2 = s2b("t2", [128, CQ], F32)
                    t3 = s2b("t3", [128, CQ], F32)
                    t4 = s2b("t4", [128, CQ], F32)
                    P2 = s2b("P2", [128, 4 * CQ], F32)
                    R3a = s2b("R3a", [128, 4 * C2], F32)
                    R3b = s2b("R3b", [128, 4 * C2], F32)
                    w3 = s2b("w3", [128, 2 * (C2 + 1)], F32)
                    winc = s2b("winc", [128, 2 * CQ], F32)
                    wq = s2b("wq", [128, 2 * CQ], F32)
                    hh = s2b("hh", [128, 4 * T], BF16)
                    hl = s2b("hl", [128, 4 * T], BF16)
                    n2 = s2b("n2", [128, T], F32)
                    sqt = s2b("sqt", [128, T], F32)

                    def htile(tb, a, jg):
                        return tb[:, (a * 2 + jg) * T:(a * 2 + jg + 1) * T]

                    FBAR()
                    s.wait("scalar", b_done)
                    s.wait("vector", b_done)
                    s.wait("vector", d_vt)
                    prev_combine = None
                    for jg in range(2):
                        ms = [Mtile(c, jg) for c in range(4)]
                        mv = [m.rearrange("p (s q) -> p s q", s=LS) for m in ms]
                        if jg == 1:
                            s.wait("scalar", [prev_combine])
                            dn = s.op("scalar", lambda e, ms=ms: e.activation(n2[:], ms[0], AF.Square), inc="sACT")
                            dadd = None
                            for c in (1, 2, 3):
                                s.wait("scalar", [dadd])
                                dq = s.op("scalar", lambda e, ms=ms, c=c: e.activation(sqt[:], ms[c], AF.Square), inc="sACT")
                                s.wait("vector", [dn, dq])
                                vdrain()
                                dadd = s.op("vector", lambda e: e.tensor_tensor(n2[:], n2[:], sqt[:], OP.add), inc="sDVE")
                            vdrain()
                            dnv = s.op("vector", lambda e: e.tensor_scalar(n2[:], n2[:], 1e-12, None, OP.max), inc="sDVE")
                            s.wait("scalar", [dnv])
                            s.op("scalar", lambda e: e.activation(n2[:], n2[:], AF.Ln))
                            dr = s.op("scalar", lambda e: e.activation(n2[:], n2[:], AF.Exp, scale=-0.5), inc="sACT")
                            s.wait("vector", [dr])
                            for c in range(4):
                                s.op("vector", lambda e, ms=ms, c=c: e.tensor_tensor(ms[c], ms[c], n2[:], OP.mult))

                            for st_ in range(1, LS):
                                cur = [v[:, st_, :] for v in mv]
                                prv = [v[:, st_ - 1, :] for v in mv]
                                s.op("vector", lambda e, cur=cur, prv=prv: e.tensor_tensor(t1[:], cur[0], prv[0], OP.mult))
                                s.op("vector", lambda e, cur=cur, prv=prv: e.tensor_tensor(t2[:], cur[1], prv[2], OP.mult))
                                s.op("vector", lambda e, cur=cur, prv=prv: e.tensor_tensor(t3[:], cur[0], prv[1], OP.mult))
                                s.op("vector", lambda e, cur=cur, prv=prv: e.tensor_tensor(t4[:], cur[1], prv[3], OP.mult))
                                s.op("vector", lambda e, cur=cur: e.tensor_tensor(cur[0], t1[:], t2[:], OP.add))
                                s.op("vector", lambda e, cur=cur: e.tensor_tensor(cur[1], t3[:], t4[:], OP.add))
                                s.op("vector", lambda e, cur=cur, prv=prv: e.tensor_tensor(t1[:], cur[2], prv[0], OP.mult))
                                s.op("vector", lambda e, cur=cur, prv=prv: e.tensor_tensor(t2[:], cur[3], prv[2], OP.mult))
                                s.op("vector", lambda e, cur=cur, prv=prv: e.tensor_tensor(t3[:], cur[2], prv[1], OP.mult))
                                s.op("vector", lambda e, cur=cur, prv=prv: e.tensor_tensor(t4[:], cur[3], prv[3], OP.mult))
                                s.op("vector", lambda e, cur=cur: e.tensor_tensor(cur[2], t1[:], t2[:], OP.add))
                                s.op("vector", lambda e, cur=cur: e.tensor_tensor(cur[3], t3[:], t4[:], OP.add))

                        for c in range(4):
                            s.op("vector", lambda e, mv=mv, c=c: e.tensor_copy(
                                P2[:, c * CQ:(c + 1) * CQ], mv[c][:, LS - 1, :]))
                        p2v = [P2[:, c * CQ:(c + 1) * CQ].rearrange("p (q2 s2) -> p q2 s2", s2=L2S)
                               for c in range(4)]
                        for st_ in range(1, L2S):
                            cur = [v[:, :, st_] for v in p2v]
                            prv = [v[:, :, st_ - 1] for v in p2v]
                            s.op("vector", lambda e, cur=cur, prv=prv: e.tensor_tensor(t1[:, 0:C2], cur[0], prv[0], OP.mult))
                            s.op("vector", lambda e, cur=cur, prv=prv: e.tensor_tensor(t2[:, 0:C2], cur[1], prv[2], OP.mult))
                            s.op("vector", lambda e, cur=cur, prv=prv: e.tensor_tensor(t3[:, 0:C2], cur[0], prv[1], OP.mult))
                            s.op("vector", lambda e, cur=cur, prv=prv: e.tensor_tensor(t4[:, 0:C2], cur[1], prv[3], OP.mult))
                            s.op("vector", lambda e, cur=cur: e.tensor_tensor(cur[0], t1[:, 0:C2], t2[:, 0:C2], OP.add))
                            s.op("vector", lambda e, cur=cur: e.tensor_tensor(cur[1], t3[:, 0:C2], t4[:, 0:C2], OP.add))
                            s.op("vector", lambda e, cur=cur, prv=prv: e.tensor_tensor(t1[:, 0:C2], cur[2], prv[0], OP.mult))
                            s.op("vector", lambda e, cur=cur, prv=prv: e.tensor_tensor(t2[:, 0:C2], cur[3], prv[2], OP.mult))
                            s.op("vector", lambda e, cur=cur, prv=prv: e.tensor_tensor(t3[:, 0:C2], cur[2], prv[1], OP.mult))
                            s.op("vector", lambda e, cur=cur, prv=prv: e.tensor_tensor(t4[:, 0:C2], cur[3], prv[3], OP.mult))
                            s.op("vector", lambda e, cur=cur: e.tensor_tensor(cur[2], t1[:, 0:C2], t2[:, 0:C2], OP.add))
                            s.op("vector", lambda e, cur=cur: e.tensor_tensor(cur[3], t3[:, 0:C2], t4[:, 0:C2], OP.add))

                        for c in range(4):
                            s.op("vector", lambda e, p2v=p2v, c=c: e.tensor_copy(
                                R3a[:, c * C2:(c + 1) * C2], p2v[c][:, :, L2S - 1]))
                        src, dst = R3a, R3b
                        st_ = 1
                        while st_ < C2:
                            n = C2 - st_
                            for c in range(4):
                                s.op("vector", lambda e, src=src, dst=dst, c=c, st_=st_: e.tensor_copy(
                                    dst[:, c * C2:c * C2 + st_], src[:, c * C2:c * C2 + st_]))

                            def A(c, lo, ln, src=src):
                                return src[:, c * C2 + lo:c * C2 + lo + ln]

                            def Bv(c, lo, ln, dst=dst):
                                return dst[:, c * C2 + lo:c * C2 + lo + ln]
                            s.op("vector", lambda e, A=A, st_=st_, n=n: e.tensor_tensor(t1[:, 0:n], A(0, st_, n), A(0, 0, n), OP.mult))
                            s.op("vector", lambda e, A=A, st_=st_, n=n: e.tensor_tensor(t2[:, 0:n], A(1, st_, n), A(2, 0, n), OP.mult))
                            s.op("vector", lambda e, A=A, st_=st_, n=n: e.tensor_tensor(t3[:, 0:n], A(0, st_, n), A(1, 0, n), OP.mult))
                            s.op("vector", lambda e, A=A, st_=st_, n=n: e.tensor_tensor(t4[:, 0:n], A(1, st_, n), A(3, 0, n), OP.mult))
                            s.op("vector", lambda e, Bv=Bv, st_=st_, n=n: e.tensor_tensor(Bv(0, st_, n), t1[:, 0:n], t2[:, 0:n], OP.add))
                            s.op("vector", lambda e, Bv=Bv, st_=st_, n=n: e.tensor_tensor(Bv(1, st_, n), t3[:, 0:n], t4[:, 0:n], OP.add))
                            s.op("vector", lambda e, A=A, st_=st_, n=n: e.tensor_tensor(t1[:, 0:n], A(2, st_, n), A(0, 0, n), OP.mult))
                            s.op("vector", lambda e, A=A, st_=st_, n=n: e.tensor_tensor(t2[:, 0:n], A(3, st_, n), A(2, 0, n), OP.mult))
                            s.op("vector", lambda e, A=A, st_=st_, n=n: e.tensor_tensor(t3[:, 0:n], A(2, st_, n), A(1, 0, n), OP.mult))
                            s.op("vector", lambda e, A=A, st_=st_, n=n: e.tensor_tensor(t4[:, 0:n], A(3, st_, n), A(3, 0, n), OP.mult))
                            s.op("vector", lambda e, Bv=Bv, st_=st_, n=n: e.tensor_tensor(Bv(2, st_, n), t1[:, 0:n], t2[:, 0:n], OP.add))
                            s.op("vector", lambda e, Bv=Bv, st_=st_, n=n: e.tensor_tensor(Bv(3, st_, n), t3[:, 0:n], t4[:, 0:n], OP.add))
                            src, dst = dst, src
                            st_ *= 2
                        R3f = src

                        v0ap = vt[:, jg:jg + 1]
                        v1ap = vt[:, 2 + jg:2 + jg + 1]
                        for a in range(2):
                            base = a * (C2 + 1)
                            vap = v0ap if a == 0 else v1ap
                            s.op("vector", lambda e, base=base, vap=vap: e.tensor_copy(
                                w3[:, base:base + 1], vap))
                            s.op("vector", lambda e, a=a, base=base, R3f=R3f, v0ap=v0ap: e.tensor_scalar(
                                w3[:, base + 1:base + 1 + C2], R3f[:, (a * 2) * C2:(a * 2 + 1) * C2],
                                v0ap, None, OP.mult))
                            vdrain()
                            s.op("vector", lambda e, a=a, base=base, R3f=R3f, v1ap=v1ap: e.scalar_tensor_tensor(
                                w3[:, base + 1:base + 1 + C2], R3f[:, (a * 2 + 1) * C2:(a * 2 + 2) * C2], v1ap,
                                w3[:, base + 1:base + 1 + C2], OP.mult, OP.add))

                        w0b = w3[:, 0:C2].unsqueeze(2).broadcast_to((128, C2, L2S))
                        w1b = w3[:, (C2 + 1):(C2 + 1) + C2].unsqueeze(2).broadcast_to((128, C2, L2S))
                        for a in range(2):
                            ov = winc[:, a * CQ:(a + 1) * CQ]
                            s.op("vector", lambda e, a=a, ov=ov, p2v=p2v, w0b=w0b: e.tensor_tensor(
                                ov.rearrange("p (q2 s2) -> p q2 s2", s2=L2S), p2v[a * 2], w0b, OP.mult))
                            s.op("vector", lambda e, a=a, p2v=p2v, w1b=w1b: e.tensor_tensor(
                                t1[:, 0:CQ].rearrange("p (q2 s2) -> p q2 s2", s2=L2S), p2v[a * 2 + 1], w1b, OP.mult))
                            vdrain()
                            s.op("vector", lambda e, ov=ov: e.tensor_tensor(ov, ov, t1[:, 0:CQ], OP.add))

                        for a in range(2):
                            base = a * CQ
                            s.op("vector", lambda e, a=a, base=base: e.tensor_copy(
                                wq[:, base + 1:base + CQ], winc[:, a * CQ:a * CQ + CQ - 1]))
                            s.op("vector", lambda e, a=a, base=base: e.tensor_copy(
                                wq[:, base:base + 1], w3[:, a * (C2 + 1):a * (C2 + 1) + 1]))
                            s.op("vector", lambda e, a=a, base=base: e.tensor_copy(
                                wq[:, base:base + CQ].rearrange("p (q2 s2) -> p q2 s2", s2=L2S)[:, 1:C2, 0],
                                w3[:, a * (C2 + 1) + 1:a * (C2 + 1) + C2]))

                        w0bT = wq[:, 0:CQ].unsqueeze(1).broadcast_to((128, LS, CQ))
                        w1bT = wq[:, CQ:2 * CQ].unsqueeze(1).broadcast_to((128, LS, CQ))
                        for a in range(2):
                            hv = n2[:].rearrange("p (s q) -> p s q", s=LS)
                            qv = sqt[:].rearrange("p (s q) -> p s q", s=LS)
                            s.op("vector", lambda e, a=a, mv=mv, hv=hv, w0bT=w0bT: e.tensor_tensor(
                                hv, mv[a * 2], w0bT, OP.mult))
                            s.op("vector", lambda e, a=a, mv=mv, qv=qv, w1bT=w1bT: e.tensor_tensor(
                                qv, mv[a * 2 + 1], w1bT, OP.mult))
                            vdrain()
                            s.op("vector", lambda e: e.tensor_tensor(n2[:], n2[:], sqt[:], OP.add))
                            vdrain()
                            s.op("vector", lambda e, a=a, jg=jg: e.tensor_copy(htile(hh, a, jg), n2[:]))
                            vdrain()
                            dcomb = s.op("vector", lambda e, a=a, jg=jg: e.tensor_tensor(
                                htile(hl, a, jg), n2[:], htile(hh, a, jg), OP.subtract), inc="sDVE")
                        prev_combine = dcomb
                    hs_done = prev_combine

                    if dbg and se == 0:
                        s.wait("gpsimd", [hs_done, s.val("sACT")])
                        s.op("gpsimd", lambda e: e.dma_start(out=dM, in_=Mt[:]), inc="sDBG", n=16)
                        s.op("gpsimd", lambda e: e.dma_start(out=dH, in_=hh[:]), inc="sDBG", n=16)

                    s.wait("tensor", [hs_done, w_ready])
                    s.wait("scalar", [hs_done])
                    aotP = [n2[:, 0:D], n2[:, D:2 * D]]
                    ln2tP = [sqt[:, 0:D], sqt[:, D:2 * D]]
                    prev_dao = [None, None]
                    prev_dln = [None, None]
                    prev_dpl = [None, None]
                    prev_dx2 = [None, None]
                    for tc in range(NTC):
                        par = tc % 2
                        idx = se * 16 + tc
                        W = par * 16
                        pi = 2 + (tc % 2)
                        aot = aotP[par]
                        ln2t = ln2tP[par]
                        s.wait("sync", [xq_cons[par]])
                        dep_x = s.op("sync", lambda e, se=se, tc=tc, par=par: e.dma_start(
                            out=xq[:, par * D:(par + 1) * D],
                            in_=xr[se, tc // 2, (tc % 2) * 128:(tc % 2) * 128 + 128, :]),
                            inc="sXQa" if par == 0 else "sXQb", n=16)
                        bank_acquire(2 * pi)
                        bank_acquire(2 * pi + 1)
                        dmm = None
                        for cc in range(4):
                            jgc, ac = cc // 2, cc % 2
                            for hs_ in (hh, hl):
                                wss = (wph, wpl) if hs_ is hh else (wph,)
                                for ws in wss:
                                    firstpass = (cc == 0) and (hs_ is hh) and (ws is wph)
                                    lastpass = (cc == 3) and (hs_ is hl)
                                    for half in range(2):
                                        lastmm = lastpass and (half == 1)
                                        dmm = s.op("tensor", lambda e, cc=cc, jgc=jgc, ac=ac, hs_=hs_, ws=ws, half=half, st=firstpass, sp=lastpass, pi=pi, tc=tc: e.matmul(
                                            PS[pi][:, half * 512:half * 512 + 512],
                                            htile(hs_, ac, jgc)[:, tc * 128:tc * 128 + 128],
                                            ws[:, (jgc * 2 + ac) * D + half * 512:(jgc * 2 + ac) * D + half * 512 + 512],
                                            start=st, stop=sp), inc="sPE" if lastmm else None)
                        bank_prod[2 * pi] = dmm
                        bank_prod[2 * pi + 1] = dmm
                        s.wait("scalar", [dep_x, prev_dao[par], prev_dx2[par]])
                        dy = s.op("scalar", lambda e, par=par, idx=idx, ln2t=ln2t: e.activation(
                            ln2t, xq[:, par * D:(par + 1) * D], AF.Identity,
                            scale=stripA[:, idx:idx + 1], bias=stripB[:, idx:idx + 1]), inc="sACT")
                        s.wait("vector", [dy, bank_prod[2 * pi], prev_dln[par], prev_dpl[par]])
                        dao = s.op("vector", lambda e, pi=pi, aot=aot, ln2t=ln2t: e.tensor_tensor(
                            aot, PS[pi][:], ln2t, OP.add), inc="sDVE")
                        bank_cons[2 * pi].append(dao)
                        bank_cons[2 * pi + 1].append(dao)
                        s.wait("scalar", [dao])
                        dvs, a2ap, b2ap = ln_strips(aot, W, idx, "plain")
                        s.wait("scalar", [dvs])
                        dln = s.op("scalar", lambda e, a2ap=a2ap, b2ap=b2ap, aot=aot, ln2t=ln2t: e.activation(
                            ln2t, aot, AF.Identity, scale=a2ap, bias=b2ap), inc="sACT")
                        s.wait("vector", [dln])
                        dx2 = s.op("vector", lambda e, par=par, aot=aot, ln2t=ln2t: e.tensor_tensor(
                            aot, xq[:, par * D:(par + 1) * D], ln2t, OP.add), inc="sDVE")
                        xq_cons[par] = dx2
                        s.wait("gpsimd", [dx2])
                        dst_row = se * T + tc * 128
                        dpl = s.op("gpsimd", lambda e, dst_row=dst_row, aot=aot: e.dma_start(
                            out=x2s[dst_row:dst_row + 128, :], in_=aot), inc="sX2W", n=16)
                        x2w[idx] = dpl
                        prev_dao[par], prev_dln[par], prev_dpl[par], prev_dx2[par] = dao, dln, dpl, dx2

        attn_pe = s.val("sPE")

        with ExitStack() as ffn:
            def fsb(name, shape, dt):
                return ffn.enter_context(nc.sbuf_tensor(name, shape, dt))
            h0h = fsb("h0h", [128, 32 * 512], BF16)
            h0l = fsb("h0l", [128, 32 * 512], BF16)
            h1h = fsb("h1h", [128, 32 * 512], BF16)
            h1l = fsb("h1l", [128, 32 * 512], BF16)
            xn3h = fsb("xn3h", [128, 8 * 512], BF16)
            xn3l = fsb("xn3l", [128, 8 * 512], BF16)
            w0b2 = fsb("w0b2", [128, 4 * 512], BF16)
            w1b2 = fsb("w1b2", [128, 4 * 512], BF16)
            w2b2 = fsb("w2b2", [128, 3 * 1024], BF16)
            x2t = fsb("x2t", [128, 4 * D], F32)
            xn3t = fsb("xn3t", [128, 4 * D], F32)
            r32 = fsb("r32", [128, 2 * 512], F32)
            lno = fsb("lno", [128, D], F32)
            ot = fsb("ot", [128, D], F32)

            FBAR()
            guardF = [s.val("sPE"), s.val("sACT"), s.val("sDVE")]
            s.wait("scalar", guardF)
            s.wait("vector", guardF)
            s.wait("scalar", [d_scs])
            s.wait("sync", conv_done + [attn_pe])
            s.wait("gpsimd", [attn_pe])
            x2t_cons = [None, None, None, None]
            xn3t_cons = [None, None, None, None]
            stw2 = fsb("stw2", [128, 32], F32)
            f_dvs = [None, None]
            f_dxn = [None, None]
            pend_front = {}
            pend_load = {}
            prev_dx3 = [None]
            wb_cons = {"w0": [None]*4, "w1": [None]*4, "w2": [None]*4}
            W0S_ = ("sW0a","sW0b","sW0c","sW0d"); W1S_ = ("sW1a","sW1b","sW1c","sW1d"); W2S_ = ("sW2a","sW2b","sW2c","sW2d")
            ot_cons = [None, None]
            r32_cons = [None, None]
            pend_fe = [None]

            for ft in range(NFT):
                se = ft // 4
                BAR("ft")
                def backF_evac(k):
                    base = (k % 2) * 2
                    src = PSF[:, base * 512:(base + 2) * 512].rearrange(
                        "p (b c t) -> p b c t", b=2, c=4)
                    dsth = xn3h[:].rearrange("p (b c t) -> p b c t", b=2, c=4)[:, :, :, k * 128:k * 128 + 128]
                    dstl = xn3l[:].rearrange("p (b c t) -> p b c t", b=2, c=4)[:, :, :, k * 128:k * 128 + 128]
                    s.wait("scalar", [bank_prod[base + 1]])
                    dhi = s.op("scalar", lambda e, dsth=dsth, src=src: e.copy(dsth, src), inc="sACT")
                    s.wait("vector", [dhi])
                    dlo = s.op("vector", lambda e, dstl=dstl, src=src, dsth=dsth: e.tensor_tensor(
                        dstl, src, dsth, OP.subtract), inc="sDVE")
                    bank_cons[base] = [dhi, dlo]
                    bank_cons[base + 1] = [dhi, dlo]
                    return dhi, dlo

                def loadF(fti, k):
                    gp = fti * 512 + k * 128
                    s.wait("gpsimd", [x2t_cons[k], x2w[gp // 128]])
                    dxl = s.op("gpsimd", lambda e, gp=gp, k=k: e.dma_start(
                        out=x2t[:, k * D:(k + 1) * D], in_=x2s[gp:gp + 128, :]),
                        inc="sX2a" if k % 2 == 0 else "sX2b", n=16)
                    pend_load[(fti, k)] = dxl

                def frontF(fti, k, SB, mid=None):
                    par = k % 2
                    dxl = pend_load.pop((fti, k))
                    s.wait("scalar", [dxl])
                    s.wait("vector", [f_dvs[par], f_dxn[par]])
                    dvs, a3ap, b3ap = ln_strips(x2t[:, k * D:(k + 1) * D], par * 16, 0, "plain",
                                                SB=SB, mid=mid)
                    f_dvs[par] = dvs
                    s.wait("scalar", [dvs, xn3t_cons[k]])
                    dxn = s.op("scalar", lambda e, k=k, a3ap=a3ap, b3ap=b3ap: e.activation(
                        xn3t[:, k * D:(k + 1) * D], x2t[:, k * D:(k + 1) * D],
                        AF.Identity, scale=a3ap, bias=b3ap), inc="sACT")
                    x2t_cons[k] = dxn
                    f_dxn[par] = dxn
                    pend_front[(fti, k)] = dxn

                def backF(fti, k):
                    dxn = pend_front.pop((fti, k))
                    s.wait("tensor", [dxn, d_id])
                    base = (k % 2) * 2
                    bank_acquire(base)
                    bank_acquire(base + 1)
                    dtr = None
                    for dc in range(8):
                        o = (base + dc // 4) * 512 + (dc % 4) * 128
                        dtr = s.op("tensor", lambda e, o=o, dc=dc, k=k: e.transpose(
                            PSF[:, o:o + 128], xn3t[:, k * D + dc * 128:k * D + (dc + 1) * 128], ident[:]), inc="sPE")
                    bank_prod[base] = dtr
                    bank_prod[base + 1] = dtr
                    xn3t_cons[k] = dtr

                if ft == 0:
                    for k in range(4):
                        loadF(0, k)
                    for k in range(4):
                        mid = (lambda k=k: backF_evac(k - 1)) if k > 0 else None
                        frontF(0, k, stw_g, mid=mid)
                        backF(0, k)
                    fe_hi, fe_lo = backF_evac(3)
                else:
                    fe_hi, fe_lo = pend_fe[0]

                if dbg and ft == 0:
                    s.wait("gpsimd", [s.val("sACT"), s.val("sDVE")])
                    s.op("gpsimd", lambda e: e.dma_start(out=dX3h, in_=xn3h[:]), inc="sDBG", n=16)
                    s.op("gpsimd", lambda e: e.dma_start(out=dX3l, in_=xn3l[:]), inc="sDBG", n=16)
                BAR("fg")
                xn3_done = [fe_hi, fe_lo]
                s.wait("tensor", xn3_done)
                s.wait("scalar", xn3_done)
                s.wait("vector", xn3_done)
                for grp in range(8):
                    b0 = (grp % 2) * 4
                    for b in range(4):
                        bank_acquire(b0 + b)
                    for dc in range(8):
                        par = (grp * 8 + dc) % 4
                        s.wait("sync", [wb_cons["w0"][par]])
                        dw = s.op("sync", lambda e, dc=dc, grp=grp, par=par: e.dma_start(
                            out=w0b2[:, par * 512:(par + 1) * 512],
                            in_=w0s[dc][:, grp * 512:(grp + 1) * 512]),
                            inc=W0S_[par], n=16)
                        s.wait("tensor", [dw])
                        for fi in range(4):
                            for xsp in (xn3h, xn3l):
                                lastbank = (dc == 7) and (xsp is xn3l)
                                lastbuf = (fi == 3) and (xsp is xn3l)
                                dmm = s.op("tensor", lambda e, dc=dc, fi=fi, par=par, xsp=xsp, st=(dc == 0 and xsp is xn3h), sp=lastbank, b0=b0: e.matmul(
                                    BANK(b0 + fi), w0b2[:, par * 512 + fi * 128:par * 512 + fi * 128 + 128],
                                    xsp[:, dc * 512:(dc + 1) * 512], start=st, stop=sp),
                                    inc="sPE" if (lastbank or lastbuf) else None)
                                if lastbank:
                                    bank_prod[b0 + fi] = dmm
                                if lastbuf:
                                    wb_cons["w0"][par] = dmm
                    for fi in range(4):
                        f0g = grp * 4 + fi
                        rpar = fi % 2
                        s.wait("scalar", [r32_cons[rpar]])
                        dr1 = bank_read("scalar", b0 + fi, lambda e, fi=fi, rpar=rpar, b0=b0: e.activation(
                            r32[:, rpar * 512:(rpar + 1) * 512], BANK(b0 + fi), AF.Relu, scale=scs[:, 3:4]), inc="sACT")
                        dr2 = s.op("scalar", lambda e, f0g=f0g, rpar=rpar: e.copy(
                            h0h[:, f0g * 512:(f0g + 1) * 512], r32[:, rpar * 512:(rpar + 1) * 512]), inc="sACT")
                        s.wait("vector", [dr2])
                        dlo = s.op("vector", lambda e, f0g=f0g, rpar=rpar: e.tensor_tensor(
                            h0l[:, f0g * 512:(f0g + 1) * 512], r32[:, rpar * 512:(rpar + 1) * 512],
                            h0h[:, f0g * 512:(f0g + 1) * 512], OP.subtract), inc="sDVE")
                        r32_cons[rpar] = dlo
                h0_done = s.val("sDVE")
                if dbg and ft == 0:
                    s.wait("gpsimd", [h0_done, s.val("sACT")])
                    s.op("gpsimd", lambda e: e.dma_start(out=dH0, in_=h0h[:]), inc="sDBG", n=16)

                BAR("gh")
                BARX("gh_ta", "tensor", ("sACT",))
                BARX("gh_sp", "sync", ("sPE", "sACT", "sDVE"))
                BARX("gh_v", "vector", ("sPE", "sACT"))
                BARX("gh_vp", "vector", ("sPE",))
                BARX("gh_va", "vector", ("sACT",))
                BARX("gh_a", "scalar", ("sPE", "sDVE"))
                BARX("gh_pl", "gpsimd", ("sPE", "sACT", "sDVE"))
                gh_done = [s.val("sPE"), s.val("sACT"), s.val("sDVE")]
                s.wait("tensor", gh_done)
                s.wait("scalar", gh_done)
                s.wait("vector", gh_done)
                for grp in range(8):
                    b0 = (grp % 2) * 4
                    for b in range(4):
                        bank_acquire(b0 + b)
                    for k in range(32):
                        par = (grp * 32 + k) % 4
                        s.wait("sync", [wb_cons["w1"][par]])
                        dw = s.op("sync", lambda e, k=k, grp=grp, par=par: e.dma_start(
                            out=w1b2[:, par * 512:(par + 1) * 512],
                            in_=w1s[k][:, grp * 512:(grp + 1) * 512]),
                            inc=W1S_[par], n=16)
                        s.wait("tensor", [dw])
                        for fi in range(4):
                            for xsp in (h0h, h0l):
                                lastbank = (k == 31) and (xsp is h0l)
                                lastbuf = (fi == 3) and (xsp is h0l)
                                dmm = s.op("tensor", lambda e, k=k, fi=fi, par=par, xsp=xsp, st=(k == 0 and xsp is h0h), sp=lastbank, b0=b0: e.matmul(
                                    BANK(b0 + fi), w1b2[:, par * 512 + fi * 128:par * 512 + fi * 128 + 128],
                                    xsp[:, k * 512:(k + 1) * 512], start=st, stop=sp),
                                    inc="sPE" if (lastbank or lastbuf) else None)
                                if lastbank:
                                    bank_prod[b0 + fi] = dmm
                                if lastbuf:
                                    wb_cons["w1"][par] = dmm
                    for fi in range(4):
                        f1g = grp * 4 + fi
                        rpar = fi % 2
                        s.wait("scalar", [r32_cons[rpar]])
                        dr1 = bank_read("scalar", b0 + fi, lambda e, fi=fi, rpar=rpar, b0=b0: e.activation(
                            r32[:, rpar * 512:(rpar + 1) * 512], BANK(b0 + fi), AF.Relu, scale=scs[:, 4:5]), inc="sACT")
                        dr2 = s.op("scalar", lambda e, f1g=f1g, rpar=rpar: e.copy(
                            h1h[:, f1g * 512:(f1g + 1) * 512], r32[:, rpar * 512:(rpar + 1) * 512]), inc="sACT")
                        s.wait("vector", [dr2])
                        dlo = s.op("vector", lambda e, f1g=f1g, rpar=rpar: e.tensor_tensor(
                            h1l[:, f1g * 512:(f1g + 1) * 512], r32[:, rpar * 512:(rpar + 1) * 512],
                            h1h[:, f1g * 512:(f1g + 1) * 512], OP.subtract), inc="sDVE")
                        r32_cons[rpar] = dlo
                h1_done = s.val("sDVE")
                if dbg and ft == 0:
                    s.wait("gpsimd", [h1_done, s.val("sACT")])
                    s.op("gpsimd", lambda e: e.dma_start(out=dH1, in_=h1h[:]), inc="sDBG", n=16)

                BAR("hi")
                hi_done = [s.val("sPE"), s.val("sACT"), s.val("sDVE")]
                s.wait("tensor", hi_done)
                s.wait("scalar", hi_done)
                s.wait("vector", hi_done)
                for b in range(8):
                    bank_acquire(b)

                def ImmGroup(tg):
                    dmm = None
                    for k in range(32):
                        par = (tg * 32 + k) % 3
                        s.wait("sync", [wb_cons["w2"][par]])
                        dw = s.op("sync", lambda e, k=k, par=par: e.dma_start(
                            out=w2b2[:, par * 1024:(par + 1) * 1024], in_=w2s[k]),
                            inc=W2S_[par], n=16)
                        s.wait("tensor", [dw])
                        for xsp in (h1h, h1l):
                            for tci in (2 * tg, 2 * tg + 1):
                                for half in range(2):
                                    lastbuf = (xsp is h1l) and (tci == 2 * tg + 1) and (half == 1)
                                    dmm = s.op("tensor", lambda e, k=k, par=par, xsp=xsp, tci=tci, half=half, st=(k == 0 and xsp is h1h), sp=(k == 31 and xsp is h1l): e.matmul(
                                        PS[tci][:, half * 512:half * 512 + 512],
                                        xsp[:, k * 512 + tci * 128:k * 512 + tci * 128 + 128],
                                        w2b2[:, par * 1024 + half * 512:par * 1024 + half * 512 + 512],
                                        start=st, stop=sp),
                                        inc="sPE" if lastbuf else None)
                                    if lastbuf:
                                        wb_cons["w2"][par] = dmm
                    for b in range(4 * tg, 4 * tg + 4):
                        bank_prod[b] = dmm

                dvsL = [None] * 4
                scbL = [None] * 4
                dlnL = [None] * 4

                def Ichain(tcis):
                    for tci in tcis:
                        par_ = tci % 2
                        s.wait("scalar", [bank_prod[2 * tci], dvsL[tci - 2] if tci >= 2 else None])
                        if tci >= 2:
                            s.wait("vector", [dlnL[tci - 2]])
                        dvs, sc4, b4 = ln_strips(PS[tci][:], par_ * 16, 0, "s2")
                        dvsL[tci], scbL[tci] = dvs, (sc4, b4)
                        bank_cons[2 * tci].append(dvs)
                        bank_cons[2 * tci + 1].append(dvs)
                    for tci in tcis:
                        par_ = tci % 2
                        gp = ft * 512 + tci * 128
                        sc4, b4 = scbL[tci]
                        s.wait("scalar", [dvsL[tci], prev_dx3[0]])
                        dln = s.op("scalar", lambda e, tci=tci, sc4=sc4, b4=b4: e.activation(
                            lno[:], PS[tci][:], AF.Identity, scale=sc4, bias=b4), inc="sACT")
                        bank_cons[2 * tci].append(dln)
                        bank_cons[2 * tci + 1].append(dln)
                        dlnL[tci] = dln
                        s.wait("gpsimd", [x2t_cons[tci]])
                        dxl = s.op("gpsimd", lambda e, gp=gp, tci=tci: e.dma_start(
                            out=x2t[:, tci * D:(tci + 1) * D], in_=x2s[gp:gp + 128, :]),
                            inc="sX2a" if tci % 2 == 0 else "sX2b", n=16)
                        s.wait("vector", [dln, dxl, ot_cons[0], ot_cons[1]])
                        dx3 = s.op("vector", lambda e, tci=tci: e.tensor_tensor(
                            ot[:], lno[:],
                            x2t[:, tci * D:(tci + 1) * D], OP.add), inc="sDVE")
                        x2t_cons[tci] = dx3
                        prev_dx3[0] = dx3
                        s.wait("gpsimd", [dx3])
                        pos = gp % T
                        dst = outr[se, pos // CQ, pos % CQ:pos % CQ + 128, :]
                        dout = s.op("gpsimd", lambda e, dst=dst: e.dma_start(
                            out=dst, in_=ot[:]),
                            inc="sOUTa" if par_ == 0 else "sOUTb", n=16)
                        ot_cons[par_] = dout
                        if ft + 1 < NFT:
                            loadF(ft + 1, tci)
                    if ft + 1 < NFT:
                        for tci in tcis:
                            frontF(ft + 1, tci, stw2)

                ImmGroup(0)
                Ichain((0, 1))
                ImmGroup(1)
                if ft + 1 < NFT:
                    backF(ft + 1, 0)
                    backF_evac(0)
                    backF(ft + 1, 1)
                    backF_evac(1)
                Ichain((2, 3))
                if ft + 1 < NFT:
                    backF(ft + 1, 2)
                    backF_evac(2)
                    backF(ft + 1, 3)
                    pend_fe[0] = backF_evac(3)

            s.wait("gpsimd", [s.val(x) for x in ("sOUTa", "sOUTb", "sX2W", "sPSa", "sPSb", "sPSc", "sPSd", "sPSe", "sPSf", "sPL", "sDBG")])

        with nc.Block() as block:
            @block.sync
            def _(e):
                for fn in s.q["sync"]:
                    fn(e)

            @block.scalar
            def _(e):
                for fn in s.q["scalar"]:
                    fn(e)

            @block.vector
            def _(e):
                for fn in s.q["vector"]:
                    fn(e)

            @block.tensor
            def _(e):
                for fn in s.q["tensor"]:
                    fn(e)

            @block.gpsimd
            def _(e):
                for fn in s.q["gpsimd"]:
                    fn(e)
    return nc


def make_inputs(x_core, W_m, W_proj, v_vec, out_scale, k0, k1, k2, s0, s1, s2):
    f32 = np.float32
    Wm_r = np.asarray(W_m, f32).reshape(KJ, 4, D)
    wmt = np.ascontiguousarray(Wm_r.transpose(1, 0, 2).reshape(D, D).T)
    Wp = np.asarray(W_proj, f32)
    idxs = np.empty(512, np.int64)
    for jg in range(2):
        for a in range(2):
            jj = np.arange(128)
            idxs[jg * 256 + a * 128 + jj] = (jg * 128 + jj) * 2 + a
    wpt = np.ascontiguousarray(Wp.T[idxs, :])
    vvh = np.ascontiguousarray(np.asarray(v_vec, f32).reshape(KJ, 2))
    osr = np.ascontiguousarray(np.broadcast_to(np.asarray(out_scale, f32), (128, D)))
    idin = np.eye(128, dtype=f32)
    k0u = np.asarray(k0).astype(np.uint8)
    k0t_, k0b_ = np.ascontiguousarray(k0u[:D]), np.ascontiguousarray(k0u[D:])
    k1u_ = np.ascontiguousarray(np.asarray(k1).astype(np.uint8))
    k2u_ = np.ascontiguousarray(np.asarray(k2).astype(np.uint8))
    scin = np.ascontiguousarray(np.broadcast_to(np.stack(
        [np.float32(s0), np.float32(s1), np.float32(s2)]), (128, 3)))
    return {
        "xin": np.ascontiguousarray(np.asarray(x_core, f32).reshape(NPOS, D)),
        "wmt": wmt, "wpt": wpt, "vv": vvh, "osr": osr, "idin": idin,
        "k0t": k0t_, "k0b": k0b_, "k1u": k1u_, "k2u": k2u_, "scin": scin,
    }


def kernel(L0_attn_x, L0_attn_W_m, L0_attn_v_vec, L0_attn_W_proj, L0_attn_out_scale,
           L0_ffn_k0, L0_ffn_k1, L0_ffn_k2, L0_ffn_s0, L0_ffn_s1, L0_ffn_s2):
    x = np.asarray(L0_attn_x, np.float32)
    nc = build(dbg=False)
    in_maps = []
    for c in range(8):
        in_maps.append(make_inputs(
            x[2 * c:2 * c + 2], L0_attn_W_m, L0_attn_W_proj, L0_attn_v_vec,
            L0_attn_out_scale, L0_ffn_k0, L0_ffn_k1, L0_ffn_k2,
            L0_ffn_s0, L0_ffn_s1, L0_ffn_s2))
    res = run_bass_kernel_spmd(nc, in_maps, core_ids=list(range(8)))
    outs = [res.results[c]["out"].reshape(SEQ, T, D) for c in range(8)]
    return np.ascontiguousarray(np.concatenate(outs, axis=0))
